# version 01 (final)
# baseline (speedup 1.0000x reference)
import numpy as np
import ml_dtypes
import concourse.bass as bass
import concourse.mybir as mybir
from concourse.bass import IndirectOffsetOnAxis
from concourse.bass_utils import run_bass_kernel_spmd
from contextlib import ExitStack

F32 = mybir.dt.float32
BF16 = mybir.dt.bfloat16
I32 = mybir.dt.int32
AF = mybir.ActivationFunctionType
ALU = mybir.AluOpType
AX = mybir.AxisListType
BF = ml_dtypes.bfloat16

T, C, DI, NST, DTR = 2048, 1024, 2048, 16, 64
NE, CAP, TOPK, SF = 64, 512, 8, 768
TT, CT, DT = T // 128, C // 128, DI // 128
NCORE = 8
TCH = 512
NCH = T // TCH
NTS = TCH // 128
BIG = 1.0e9


def colblocks(w, nb):
    R = w.shape[0]
    a = np.asarray(w, np.float32).reshape(R, nb, 128).transpose(1, 0, 2)
    a = a.reshape(nb, R // 128, 128, 128).transpose(0, 2, 1, 3).reshape(nb, 128, R)
    return np.ascontiguousarray(a)


def vec2d(v):
    k = v.shape[0] // 128
    return np.ascontiguousarray(np.asarray(v, np.float32).reshape(k, 128).T)


class Cx:
    def __init__(self, nc, es):
        self.nc, self.es = nc, es
        self.sems, self.cnt = {}, {}

    def sem(self, name):
        if name not in self.sems:
            self.sems[name] = self.es.enter_context(self.nc.semaphore(name))
            self.cnt[name] = 0
        return self.sems[name]

    def inc(self, name, instr, n=1):
        instr.then_inc(self.sem(name), n)
        self.cnt[name] += n
        return self.cnt[name]

    def val(self, name):
        self.sem(name)
        return self.cnt[name]

    def wait(self, eng, name, tgt=None):
        eng.wait_ge(self.sem(name), self.cnt[name] if tgt is None else tgt)

    def bump(self, eng, tag, instr):
        n = self.inc("sb_" + tag, instr)
        eng.wait_ge(self.sem("sb_" + tag), n)
        return instr


class VCx:

    def __init__(self, nc, es, gnames=()):
        self.nc, self.es = nc, es
        self.cnt, self.base, self.phys_of = {}, {}, {}
        self.phys = []
        self.last_epoch = {}
        self.epoch = 0
        self.gnames = set(gnames)

    def new_epoch(self):
        self.epoch += 1

    def _phys_for(self, name):
        if name in self.phys_of:
            return self.phys[self.phys_of[name]]
        dedicated = name.startswith("sb_") or name == "PH" or name in self.gnames
        p = None
        if not dedicated:
            for cand in self.phys:
                if cand["ded"]:
                    continue
                if all(self.last_epoch.get(n, -1) < self.epoch for n in cand["names"]):
                    p = cand
                    break
        if p is None:
            p = {"h": self.es.enter_context(self.nc.semaphore(f"cxp{len(self.phys)}")),
                 "count": 0, "names": set(), "ded": dedicated, "newest": None, "i": len(self.phys)}
            self.phys.append(p)
        self.phys_of[name] = p["i"]
        p["names"].add(name)
        p["newest"] = name
        self.base[name] = p["count"]
        self.cnt[name] = 0
        self.last_epoch[name] = self.epoch
        return p

    def sem(self, name):
        return self._phys_for(name)["h"]

    def inc(self, name, instr, n=1):
        p = self._phys_for(name)
        assert p["ded"] or p["newest"] == name, \
            f"sem {name} inc after phys reuse by {p['newest']}"
        self.last_epoch[name] = self.epoch
        p["count"] += n
        instr.then_inc(p["h"], n)
        self.cnt[name] += n
        return self.cnt[name]

    def val(self, name):
        self._phys_for(name)
        return self.cnt[name]

    def wait(self, eng, name, tgt=None):
        p = self._phys_for(name)
        t = self.cnt[name] if tgt is None else tgt
        eng.wait_ge(p["h"], self.base[name] + t)

    def bump(self, eng, tag, instr):
        n = self.inc("sb_" + tag, instr)
        eng.wait_ge(self.sem("sb_" + tag), n)
        return instr


def host_prep_A(inputs, core):
    return {
        "x": np.ascontiguousarray(np.asarray(inputs["L0_attn_x"], np.float32)[core]),
        "winh": colblocks(inputs["L0_attn_W_in"], 32),
        "conv_w2": np.ascontiguousarray(np.asarray(inputs["L0_attn_conv_w"], np.float32)
                                        .reshape(DT, 128, 4).transpose(1, 0, 2).reshape(128, DT * 4)),
        "conv_b2": vec2d(inputs["L0_attn_conv_b"]),
        "wxh": np.ascontiguousarray(np.asarray(inputs["L0_attn_W_x"], np.float32)
                                    .reshape(DT, 128, 96).transpose(1, 0, 2).reshape(128, DT * 96)),
        "W_dt": np.asarray(inputs["L0_attn_W_dt"], np.float32),
        "b_dt2": vec2d(inputs["L0_attn_b_dt"]),
        "A_log2": np.ascontiguousarray(np.asarray(inputs["L0_attn_A_log"], np.float32)
                                       .reshape(DT, 128, NST).transpose(1, 0, 2).reshape(128, DT * NST)),
        "D2": vec2d(inputs["L0_attn_D"]),
        "W_out": np.asarray(inputs["L0_attn_W_out"], np.float32),
        "router_w": np.asarray(inputs["L0_ffn_router_w"], np.float32),
        "wguh": colblocks(inputs["L0_ffn_ws_gate_up"], 12),
        "ws_down": np.asarray(inputs["L0_ffn_ws_down"], np.float32),
        "idnb": np.eye(128, dtype=BF),
        "iota64": np.broadcast_to(np.arange(64, dtype=np.float32), (128, 64)).copy(),
    }


def build_A(stage=10):
    nc = bass.Bass("TRN2", target_bir_lowering=False)
    D = nc.dram_tensor
    x = D("x", [T, C], F32, kind="ExternalInput").ap()
    winh = D("winh", [32, 128, C], F32, kind="ExternalInput").ap()
    conv_w2 = D("conv_w2", [128, DT * 4], F32, kind="ExternalInput").ap()
    conv_b2 = D("conv_b2", [128, DT], F32, kind="ExternalInput").ap()
    wxh = D("wxh", [128, DT * 96], F32, kind="ExternalInput").ap()
    W_dt = D("W_dt", [DTR, DI], F32, kind="ExternalInput").ap()
    b_dt2 = D("b_dt2", [128, DT], F32, kind="ExternalInput").ap()
    A_log2 = D("A_log2", [128, DT * NST], F32, kind="ExternalInput").ap()
    D2 = D("D2", [128, DT], F32, kind="ExternalInput").ap()
    W_out = D("W_out", [DI, C], F32, kind="ExternalInput").ap()
    router_w = D("router_w", [NE, C], F32, kind="ExternalInput").ap()
    wguh = D("wguh", [12, 128, C], F32, kind="ExternalInput").ap()
    ws_down = D("ws_down", [SF, C], F32, kind="ExternalInput").ap()
    idnb = D("idnb", [128, 128], BF16, kind="ExternalInput").ap()
    iota64 = D("iota64", [128, 64], F32, kind="ExternalInput").ap()

    r1 = D("r1", [T, C], F32, kind="ExternalOutput").ap()
    xn_out = D("xn_out", [T, C], BF16, kind="ExternalOutput").ap()
    sel_o = D("sel_o", [T, NE], BF16, kind="ExternalOutput").ap()
    cnt_o = D("cnt_o", [NE, 1], F32, kind="ExternalOutput").ap()
    wts_o = D("wts_o", [128, TT * TOPK], F32, kind="ExternalOutput").ap()
    eids_o = D("eids_o", [128, TT * TOPK], F32, kind="ExternalOutput").ap()

    bc_s = D("bc_s", [32, TCH], BF16, kind="Internal").ap()

    es = ExitStack()
    with es:
        cx = Cx(nc, es)
        sbt = lambda n, s, d: es.enter_context(nc.sbuf_tensor(n, s, d))
        block = es.enter_context(nc.Block())

        xnT = [sbt(f"xnT{i}", [128, T], BF16) for i in range(CT)]
        sidb = sbt("sidb", [128, 128], BF16)
        rwn = sbt("rwn", [128, C], BF16)
        rwTt = sbt("rwTt", [128, CT * 64], BF16)
        onesb = sbt("onesb", [128, 1], BF16)
        sel = [sbt(f"sel{i}", [128, NE], BF16) for i in range(TT)]
        wts = sbt("wts", [128, TT * TOPK], F32)
        eids = sbt("eids", [128, TT * TOPK], F32)
        convw = sbt("convw", [128, DT * 4], F32)
        convb = sbt("convb", [128, DT], F32)
        bdt = sbt("bdt", [128, DT], F32)
        alog = sbt("alog", [128, DT * NST], F32)
        aneg = sbt("aneg", [128, DT * NST], F32)
        dvec = sbt("dvec", [128, DT], F32)
        wxT = sbt("wxT", [128, DT * 96], BF16)
        wdt_b = sbt("wdt_b", [64, DI], BF16)
        u_raw = [sbt(f"uraw{i}", [128, TCH + 4], BF16) for i in range(DT)]
        u_cv = [sbt(f"ucv{i}", [128, TCH], BF16) for i in range(DT)]
        res_t = [sbt(f"res{i}", [128, TCH], BF16) for i in range(DT)]
        dl_t = [sbt(f"dl{i}", [128, TCH], BF16) for i in range(DT)]
        dlu_t = [sbt(f"dlu{i}", [128, TCH], BF16) for i in range(DT)]
        y_t = [u_raw[i][:, 4:4 + TCH] for i in range(DT)]
        carry = sbt("carry", [128, DT * NST], F32)
        ccv3 = [sbt(f"ccv3_{i}", [128, 3], BF16) for i in range(DT)]
        bbc = [sbt(f"bbc{i}", [128, TCH], BF16) for i in range(2)]
        cbc = [sbt(f"cbc{i}", [128, TCH], BF16) for i in range(2)]
        scr_a = [sbt(f"scra{i}", [128, TCH], BF16) for i in range(2)]
        scr_b = [sbt(f"scrb{i}", [128, TCH], BF16) for i in range(2)]
        scr_c = [sbt(f"scrc{i}", [128, TCH], BF16) for i in range(2)]
        scr_h = sbt("scr_h", [128, TCH], F32)
        xdT = sbt("xdT", [96, TCH], F32)
        xdTb = sbt("xdTb", [96, TCH], BF16)
        winb = [sbt(f"winb{i}", [128, C], BF16) for i in range(2)]
        wob = [sbt(f"wob{i}", [128, C], BF16) for i in range(2)]
        wgu = [sbt(f"wgu{i}", [128, C], BF16) for i in range(4)]
        wsd = [sbt(f"wsd{i}", [128, C], BF16) for i in range(2)]
        shT = [sbt(f"shT{i}", [128, TCH], BF16) for i in range(6)]
        shsc = [sbt(f"shsc{i}", [128, TCH], BF16) for i in range(2)]
        xtile = [sbt(f"xtile{i}", [128, C], F32) for i in range(2)]
        xntile = [sbt(f"xntile{i}", [128, C], BF16) for i in range(2)]
        sqfull = sbt("sqfull", [128, C], BF16)
        stat = sbt("stat", [128, 12], F32)
        logit = sbt("logit", [128, NE], F32)
        ohk = sbt("ohk", [128, NE], F32)
        io64 = sbt("io64", [128, NE], F32)
        ajunk = sbt("ajunk", [128, NE], F32)
        acc8 = sbt("acc8", [128, NE], F32)
        mcol = sbt("mcol", [128, TOPK], F32)
        scol = sbt("scol", [128, 4], F32)
        cntv = sbt("cntv", [64, 1], F32)
        r1x = [sbt(f"r1x{i}", [128, C], F32) for i in range(2)]

        @block.sync
        def _(sy):
            cx.inc("LDW", sy.dma_start(out=sidb[:], in_=idnb), 16)
            cx.inc("LDW", sy.dma_start(out=io64[:], in_=iota64), 16)
            cx.inc("LDW", sy.dma_start(out=convw[:], in_=conv_w2), 16)
            cx.inc("LDW", sy.dma_start(out=convb[:], in_=conv_b2), 16)
            cx.inc("LDW", sy.dma_start(out=bdt[:], in_=b_dt2), 16)
            cx.inc("LDW", sy.dma_start(out=alog[:], in_=A_log2), 16)
            cx.inc("LDW", sy.dma_start(out=dvec[:], in_=D2), 16)

        @block.gpsimd
        def _(gp):
            cx.inc("LDG", gp.dma_start(out=wdt_b[:], in_=W_dt), 16)
            cx.inc("LDG", gp.dma_start(out=wxT[:], in_=wxh), 16)
            cx.inc("LDG", gp.dma_start(out=rwn[0:64, :], in_=router_w), 16)
        P0 = cx.val("LDW")
        P0G = cx.val("LDG")

        @block.vector
        def _(ve):
            cx.wait(ve, "LDW", P0)
            cx.wait(ve, "LDG", P0G)
            ve.memset(rwn[64:128, :], 0.0)
            ve.memset(onesb[:], 1.0)
            ve.memset(carry[:], 0.0)
            for dt in range(DT - 1):
                ve.memset(ccv3[dt][:], 0.0)
            cx.inc("INITv", ve.memset(ccv3[DT - 1][:], 0.0))

        @block.scalar
        def _(sc):
            cx.wait(sc, "LDW", P0)
            cx.wait(sc, "LDG", P0G)
            sc.activation(out=aneg[:], in_=alog[:], func=AF.Exp)
            cx.inc("INITs", sc.mul(out=aneg[:], in_=aneg[:], mul=-1.0))

        with nc.psum_tensor("ptp0", [128, 1024], BF16) as ptp0, \
             nc.psum_tensor("ptp1", [128, 1024], BF16) as ptp1:
            ptp = [ptp0, ptp1]
            for tt in range(TT):
                pb = tt % 2
                @block.sync
                def _(sy, tt=tt, pb=pb):
                    if tt >= 2:
                        cx.wait(sy, "LN1", tt - 1)
                    cx.inc("LDX", sy.dma_start(out=xtile[pb][:], in_=x[tt * 128:(tt + 1) * 128, :]), 16)

                @block.scalar
                def _(sc, tt=tt, pb=pb):
                    cx.wait(sc, "LDX", 16 * (tt + 1))
                    if tt >= 1:
                        cx.wait(sc, "LNv", tt)
                    cx.inc("LNsq", sc.activation(out=sqfull[:], in_=xtile[pb][:], func=AF.Square))

                @block.vector
                def _(ve, tt=tt, pb=pb):
                    cx.wait(ve, "LNsq")
                    cx.bump(ve, "v", ve.reduce_sum(out=stat[:, 0:1], in_=xtile[pb][:], axis=AX.X))
                    cx.bump(ve, "v", ve.reduce_sum(out=stat[:, 1:2], in_=sqfull[:], axis=AX.X))
                    cx.bump(ve, "v", ve.tensor_scalar_mul(out=stat[:, 0:1], in0=stat[:, 0:1], scalar1=1.0 / C))
                    cx.bump(ve, "v", ve.tensor_scalar_mul(out=stat[:, 1:2], in0=stat[:, 1:2], scalar1=1.0 / C))
                    cx.bump(ve, "v", ve.tensor_tensor(out=stat[:, 8:9], in0=stat[:, 0:1], in1=stat[:, 0:1], op=ALU.mult))
                    cx.bump(ve, "v", ve.tensor_tensor(out=stat[:, 3:4], in0=stat[:, 1:2], in1=stat[:, 8:9], op=ALU.subtract))
                    cx.bump(ve, "v", ve.tensor_scalar_add(out=stat[:, 3:4], in0=stat[:, 3:4], scalar1=1e-5))
                    cx.inc("LNv", ve.engine_nop())

                @block.scalar
                def _(sc, tt=tt, pb=pb):
                    cx.wait(sc, "LNv")
                    cx.bump(sc, "s", sc.activation(out=stat[:, 4:5], in_=stat[:, 3:4], func=AF.Ln))
                    cx.bump(sc, "s", sc.activation(out=stat[:, 5:6], in_=stat[:, 4:5], func=AF.Exp, scale=-0.5))
                    cx.inc("LNr", sc.nop())

                @block.vector
                def _(ve, tt=tt, pb=pb):
                    cx.wait(ve, "LNr")
                    cx.bump(ve, "v", ve.tensor_tensor(out=stat[:, 6:7], in0=stat[:, 0:1], in1=stat[:, 5:6], op=ALU.mult))
                    cx.bump(ve, "v", ve.tensor_scalar_mul(out=stat[:, 6:7], in0=stat[:, 6:7], scalar1=-1.0))
                    cx.inc("LNb", ve.engine_nop())

                @block.scalar
                def _(sc, tt=tt, pb=pb):
                    cx.wait(sc, "LNb")
                    if tt >= 2:
                        cx.wait(sc, "STXN", 16 * (tt - 1))
                        cx.wait(sc, "P1t", (tt - 1) * CT)
                    cx.inc("LN1", sc.activation(out=xntile[pb][:], in_=xtile[pb][:], func=AF.Identity,
                                                bias=stat[:, 6:7], scale=stat[:, 5:6]))

                @block.sync
                def _(sy, tt=tt, pb=pb):
                    cx.wait(sy, "LN1")
                    cx.inc("STXN", sy.dma_start(out=xn_out[tt * 128:(tt + 1) * 128, :], in_=xntile[pb][:]), 16)

                @block.tensor
                def _(te, tt=tt, pb=pb):
                    cx.wait(te, "LN1")
                    if tt == 0:
                        cx.wait(te, "LDW", P0)
                    for ct in range(CT):
                        gi = tt * CT + ct
                        if gi >= 2:
                            cx.wait(te, "P1e", gi - 1)
                        cx.inc("P1t", te.transpose(out=ptp[ct % 2][:, 0:128],
                                     in_=xntile[pb][:, ct * 128:(ct + 1) * 128], identity=sidb[:]))

                @block.vector
                def _(ve, tt=tt, pb=pb):
                    for ct in range(CT):
                        cx.wait(ve, "P1t", tt * CT + ct + 1)
                        cx.inc("P1e", ve.tensor_copy(out=xnT[ct][:, tt * 128:(tt + 1) * 128],
                                                     in_=ptp[ct % 2][:, 0:128]))

        if stage < 1.2:
            return nc
        with nc.psum_tensor("plog", [128, 512], F32) as plogF, \
             nc.psum_tensor("prt", [128, 1024], BF16) as prtF, \
             nc.psum_tensor("pcnt", [128, 512], F32) as pcntF:
            plog = plogF[:, 0:64]
            prt = prtF[:, 0:128]
            pcnt = pcntF[0:64, 0:1]
            for ct in range(CT):
                @block.tensor
                def _(te, ct=ct):
                    if ct == 0:
                        cx.wait(te, "INITv", 1)
                        cx.wait(te, "P1e", TT * CT)
                    else:
                        cx.wait(te, "P2e", ct)
                    cx.inc("P2t", te.transpose(out=prt, in_=rwn[:, ct * 128:(ct + 1) * 128], identity=sidb[:]))

                @block.vector
                def _(ve, ct=ct):
                    cx.wait(ve, "P2t", ct + 1)
                    cx.inc("P2e", ve.tensor_copy(out=rwTt[:, ct * 64:(ct + 1) * 64], in_=prtF[:, 0:64]))

            for tt in range(TT if stage >= 1.5 else 0):
                @block.tensor
                def _(te, tt=tt):
                    cx.wait(te, "P1e", (tt + 1) * CT)
                    if tt == 0:
                        cx.wait(te, "P2e", CT)
                    else:
                        cx.wait(te, "P2l", tt)
                    for ct in range(CT):
                        mm = te.matmul(out=plog, lhsT=xnT[ct][:, tt * 128:(tt + 1) * 128],
                                       rhs=rwTt[:, ct * 64:(ct + 1) * 64], start=(ct == 0), stop=(ct == CT - 1))
                    cx.inc("P2m", mm)

                @block.vector
                def _(ve, tt=tt):
                    cx.wait(ve, "P2m", tt + 1)
                    cx.inc("P2l", ve.tensor_copy(out=logit[:], in_=plog))
                    ve.memset(acc8[:], 0.0)
                    for k in range(TOPK if stage >= 1.7 else 0):
                        cx.bump(ve, "v", ve.reduce_max(out=mcol[:, k:k + 1], in_=logit[:], axis=AX.X))
                        cx.bump(ve, "v", ve.tensor_scalar(out=ohk[:], in0=logit[:], scalar1=mcol[:, k:k + 1],
                                                          scalar2=None, op0=ALU.is_equal))
                        cx.bump(ve, "v", ve.tensor_tensor(out=ajunk[:], in0=ohk[:], in1=io64[:], op=ALU.mult))
                        ve.reduce_sum(out=eids[:, tt * TOPK + k:tt * TOPK + k + 1], in_=ajunk[:], axis=AX.X)
                        cx.bump(ve, "v", ve.tensor_tensor(out=acc8[:], in0=acc8[:], in1=ohk[:], op=ALU.add))
                        if k < TOPK - 1:
                            cx.bump(ve, "v", ve.tensor_scalar_mul(out=ohk[:], in0=ohk[:], scalar1=BIG))
                            cx.bump(ve, "v", ve.tensor_tensor(out=logit[:], in0=logit[:], in1=ohk[:], op=ALU.subtract))
                    if stage >= 1.7:
                        cx.inc("P2k", ve.tensor_copy(out=sel[tt][:], in_=acc8[:]))
                        cx.inc("P2neg", ve.tensor_scalar_mul(out=stat[:, 7:8], in0=mcol[:, 0:1], scalar1=-1.0))
                    else:
                        cx.inc("P2k", ve.engine_nop())

                @block.scalar
                def _(sc, tt=tt):
                    if stage < 1.7:
                        return
                    cx.wait(sc, "P2neg", tt + 1)
                    cx.inc("P2w", sc.activation(out=wts[:, tt * TOPK:(tt + 1) * TOPK], in_=mcol[:],
                                                func=AF.Exp, bias=stat[:, 7:8], scale=1.0))

                @block.vector
                def _(ve, tt=tt):
                    if stage < 1.7:
                        return
                    cx.wait(ve, "P2w", tt + 1)
                    cx.bump(ve, "v", ve.reduce_sum(out=scol[:, 0:1], in_=wts[:, tt * TOPK:(tt + 1) * TOPK], axis=AX.X))
                    cx.bump(ve, "v", ve.reciprocal(out=scol[:, 1:2], in_=scol[:, 0:1]))
                    cx.inc("P2wd", ve.tensor_scalar(out=wts[:, tt * TOPK:(tt + 1) * TOPK],
                                                    in0=wts[:, tt * TOPK:(tt + 1) * TOPK],
                                                    scalar1=scol[:, 1:2], scalar2=None, op0=ALU.mult))

                @block.tensor
                def _(te, tt=tt):
                    if stage < 1.9:
                        return
                    cx.wait(te, "P2k", tt + 1)
                    cx.inc("P2c", te.matmul(out=pcnt, lhsT=sel[tt][:], rhs=onesb[:],
                              start=(tt == 0), stop=(tt == TT - 1)))

            @block.vector
            def _(ve):
                if stage < 1.9:
                    return
                cx.wait(ve, "P2c", TT)
                cx.inc("P2cnt", ve.tensor_copy(out=cntv[:], in_=pcnt))

            @block.sync
            def _(sy):
                if stage < 1.9:
                    return
                cx.wait(sy, "P2cnt", 1)
                cx.wait(sy, "P2wd", TT)
                cx.inc("STR", sy.dma_start(out=cnt_o, in_=cntv[:]), 16)
                cx.inc("STR", sy.dma_start(out=wts_o, in_=wts[:]), 16)
                cx.inc("STR", sy.dma_start(out=eids_o, in_=eids[:]), 16)
                for tt in range(TT):
                    cx.inc("STR", sy.dma_start(out=sel_o[tt * 128:(tt + 1) * 128, :], in_=sel[tt][:]), 16)

        if stage < 2.2:
            return nc
        for ch in range(NCH if stage >= 10 else 1):
            t0 = ch * TCH
            tsl = slice(t0, t0 + TCH)
            with nc.psum_tensor(f"pw1a{ch}", [128, TCH], F32) as pwa, \
                 nc.psum_tensor(f"pw1b{ch}", [128, TCH], F32) as pwb:
                pw = [pwa, pwb]
                for db in range(32):
                    @block.gpsimd
                    def _(gp, db=db, ch=ch):
                        gi = ch * 32 + db
                        if gi >= 2 and stage >= 2.6:
                            cx.wait(gp, "W1m", gi - 1)
                        cx.inc("LW1", gp.dma_start(out=winb[db % 2][:], in_=winh[db]), 16)

                    @block.tensor
                    def _(te, db=db, ch=ch):
                        if stage < 2.6:
                            return
                        gi = ch * 32 + db
                        cx.wait(te, "LW1", 16 * (gi + 1))
                        if gi >= 2:
                            cx.wait(te, "W1e", gi - 1)
                        if ch == 0 and db == 0:
                            cx.wait(te, "P1e", TT * CT)
                            cx.wait(te, "P2c", TT)
                            cx.wait(te, "P2l", TT)
                            cx.wait(te, "P2cnt", 1)
                        if db == 0 and ch > 0:
                            cx.wait(te, "R1v", ch * NTS)
                        for ct in range(CT):
                            mm = te.matmul(out=pw[db % 2][:], lhsT=winb[db % 2][:, ct * 128:(ct + 1) * 128],
                                           rhs=xnT[ct][:, tsl], start=(ct == 0), stop=(ct == CT - 1))
                        cx.inc("W1m", mm)

                    @block.scalar
                    def _(sc, db=db, ch=ch):
                        if stage < 2.6:
                            return
                        gi = ch * 32 + db
                        cx.wait(sc, "W1m", gi + 1)
                        if db < DT:
                            dst = u_raw[db][:, 4:4 + TCH]
                        else:
                            dst = res_t[db - DT][:]
                        if stage < 2.8:
                            cx.inc("W1e", sc.nop())
                        elif stage < 2.87:
                            cx.inc("W1e", sc.copy(out=dst, in_=pw[db % 2][:]) if db >= DT else sc.nop())
                        elif stage < 2.9:
                            alt = dst if db >= DT else u_cv[db][:]
                            cx.inc("W1e", sc.copy(out=alt, in_=pw[db % 2][:]))
                        else:
                            cx.inc("W1e", sc.activation(out=dst, in_=pw[db % 2][:], func=AF.Identity))

            if stage < 4:
                continue
            for dt in range(DT):
                w4 = lambda j, dt=dt: convw[:, dt * 4 + j:dt * 4 + j + 1]
                @block.vector
                def _(ve, dt=dt, ch=ch):
                    cx.wait(ve, "W1e", ch * 32 + dt + 1)
                    if ch * DT + dt >= 2:
                        cx.wait(ve, "CVs", ch * DT + dt - 1)
                    s1, s2 = scr_b[dt % 2], scr_c[dt % 2]
                    ve.tensor_copy(out=u_raw[dt][:, 1:4], in_=ccv3[dt][:])
                    ve.tensor_scalar(out=s1[:], in0=u_raw[dt][:, 1:1 + TCH], scalar1=w4(0), scalar2=None, op0=ALU.mult)
                    ve.scalar_tensor_tensor(out=s2[:], in0=u_raw[dt][:, 2:2 + TCH], scalar=w4(1), in1=s1[:],
                                            op0=ALU.mult, op1=ALU.add)
                    ve.scalar_tensor_tensor(out=s1[:], in0=u_raw[dt][:, 3:3 + TCH], scalar=w4(2), in1=s2[:],
                                            op0=ALU.mult, op1=ALU.add)
                    ve.scalar_tensor_tensor(out=s2[:], in0=u_raw[dt][:, 4:4 + TCH], scalar=w4(3), in1=s1[:],
                                            op0=ALU.mult, op1=ALU.add)
                    cx.inc("CVv", ve.tensor_copy(out=ccv3[dt][:], in_=u_raw[dt][:, TCH + 1:TCH + 4]))

                @block.scalar
                def _(sc, dt=dt, ch=ch):
                    cx.wait(sc, "CVv", ch * DT + dt + 1)
                    cx.inc("CVs", sc.activation(out=u_cv[dt][:], in_=scr_c[dt % 2][:], func=AF.Silu,
                                                bias=convb[:, dt:dt + 1], scale=1.0))

            if stage < 5:
                continue
            with nc.psum_tensor(f"pxd{ch}", [96, TCH], F32) as pxd:
                @block.tensor
                def _(te, ch=ch):
                    cx.wait(te, "CVs", (ch + 1) * DT)
                    cx.wait(te, "W1e", (ch + 1) * 32)
                    for dt in range(DT):
                        mm = te.matmul(out=pxd[:], lhsT=wxT[:, dt * 96:(dt + 1) * 96], rhs=u_cv[dt][:],
                                       start=(dt == 0), stop=(dt == DT - 1))
                    cx.inc("XDm", mm)

                @block.vector
                def _(ve, ch=ch):
                    cx.wait(ve, "XDm", ch + 1)
                    cx.inc("XDe", ve.tensor_copy(out=xdTb[:], in_=pxd[:]))

            @block.sync
            def _(sy, ch=ch):
                cx.wait(sy, "XDe", ch + 1)
                cx.inc("STBC", sy.dma_start(out=bc_s, in_=xdTb[64:96, :]), 16)

            if stage < 6:
                continue
            with nc.psum_tensor(f"pdl0{ch}", [128, TCH], F32) as pd0, \
                 nc.psum_tensor(f"pdl1{ch}", [128, TCH], F32) as pd1:
                pd = [pd0, pd1]
                for dt in range(DT):
                    @block.tensor
                    def _(te, dt=dt, ch=ch):
                        gi = ch * DT + dt
                        if dt == 0:
                            cx.wait(te, "XDe", ch + 1)
                        if gi >= 2:
                            cx.wait(te, "DLs", gi - 1)
                        cx.inc("DLm", te.matmul(out=pd[dt % 2][:], lhsT=wdt_b[:, dt * 128:(dt + 1) * 128],
                                  rhs=xdTb[0:64, :], start=True, stop=True))

                    @block.scalar
                    def _(sc, dt=dt, ch=ch):
                        gi = ch * DT + dt
                        cx.wait(sc, "DLm", gi + 1)
                        sc.activation(out=scr_a[dt % 2][:], in_=pd[dt % 2][:], func=AF.Exp,
                                      bias=bdt[:, dt:dt + 1], scale=1.0)
                        cx.inc("DLs", sc.activation(out=dl_t[dt][:], in_=scr_a[dt % 2][:], func=AF.Ln,
                                                    bias=1.0, scale=1.0))

                    @block.vector
                    def _(ve, dt=dt, ch=ch):
                        cx.wait(ve, "DLs", ch * DT + dt + 1)
                        if ch > 0 and dt == 0:
                            cx.wait(ve, "WOm", ch * DT)
                        cx.inc("DLu", ve.tensor_tensor(out=dlu_t[dt][:], in0=dl_t[dt][:], in1=u_cv[dt][:],
                                                       op=ALU.mult))

            if stage < 7:
                continue
            for n in range(NST):
                @block.sync
                def _(sy, n=n, ch=ch):
                    cx.wait(sy, "STBC", 16 * (ch + 1))
                    if ch * NST + n >= 2:
                        cx.wait(sy, "SCv", (ch * NST + n - 1) * DT)
                    cx.inc("LBC", sy.dma_start(out=bbc[n % 2][:], in_=bc_s[n:n + 1, :].broadcast_to([128, TCH])), 16)
                    cx.inc("LBC", sy.dma_start(out=cbc[n % 2][:], in_=bc_s[NST + n:NST + n + 1, :].broadcast_to([128, TCH])), 16)

                for dt in range(DT):
                    it = (ch * NST + n) * DT + dt
                    @block.scalar
                    def _(sc, n=n, dt=dt, it=it, ch=ch):
                        if dt == 0 and n == 0:
                            cx.wait(sc, "DLu", (ch + 1) * DT)
                        if it >= 2:
                            cx.wait(sc, "SCv", it - 1)
                        cx.inc("SCs", sc.activation(out=scr_a[it % 2][:], in_=dl_t[dt][:], func=AF.Exp,
                                                    scale=aneg[:, dt * NST + n:dt * NST + n + 1]))

                    @block.vector
                    def _(ve, n=n, dt=dt, it=it, ch=ch):
                        cx.wait(ve, "SCs", it + 1)
                        cx.wait(ve, "LBC", 32 * (ch * NST + n + 1))
                        col = carry[:, dt * NST + n:dt * NST + n + 1]
                        ve.tensor_tensor(out=scr_b[it % 2][:], in0=dlu_t[dt][:], in1=bbc[n % 2][:], op=ALU.mult)
                        ve.tensor_tensor_scan(out=scr_h[:], data0=scr_a[it % 2][:], data1=scr_b[it % 2][:],
                                              initial=col, op0=ALU.mult, op1=ALU.add)
                        ve.tensor_copy(out=col, in_=scr_h[:, TCH - 1:TCH])
                        if n == 0:
                            cx.inc("SCv", ve.tensor_tensor(out=y_t[dt], in0=scr_h[:], in1=cbc[n % 2][:], op=ALU.mult))
                        else:
                            ve.tensor_tensor(out=scr_c[it % 2][:], in0=scr_h[:], in1=cbc[n % 2][:], op=ALU.mult)
                            cx.inc("SCv", ve.tensor_tensor(out=y_t[dt], in0=y_t[dt], in1=scr_c[it % 2][:], op=ALU.add))

            if stage < 8:
                continue
            for dt in range(DT):
                @block.scalar
                def _(sc, dt=dt, ch=ch):
                    gi = ch * DT + dt
                    if dt == 0:
                        cx.wait(sc, "SCv", (ch + 1) * NST * DT)
                    if gi >= 2:
                        cx.wait(sc, "GAv", gi - 1)
                    cx.inc("GAs", sc.activation(out=scr_a[dt % 2][:], in_=res_t[dt][:], func=AF.Silu))

                @block.vector
                def _(ve, dt=dt, ch=ch):
                    cx.wait(ve, "GAs", ch * DT + dt + 1)
                    ve.scalar_tensor_tensor(out=scr_b[dt % 2][:], in0=u_cv[dt][:], scalar=dvec[:, dt:dt + 1],
                                            in1=y_t[dt], op0=ALU.mult, op1=ALU.add)
                    cx.inc("GAv", ve.tensor_tensor(out=dlu_t[dt][:], in0=scr_b[dt % 2][:], in1=scr_a[dt % 2][:], op=ALU.mult))

            if stage < 9:
                continue
            with nc.psum_tensor(f"psg{ch}", [128, TCH], F32) as psg, \
                 nc.psum_tensor(f"psu{ch}", [128, TCH], F32) as psu:
                for i in range(6):
                    @block.gpsimd
                    def _(gp, i=i, ch=ch):
                        gi = ch * 6 + i
                        if gi >= 2:
                            cx.wait(gp, "SHm", 2 * (gi - 1))
                        cx.inc("LSH", gp.dma_start(out=wgu[(i % 2) * 2][:], in_=wguh[i]), 16)
                        cx.inc("LSH", gp.dma_start(out=wgu[(i % 2) * 2 + 1][:], in_=wguh[6 + i]), 16)

                    @block.tensor
                    def _(te, i=i, ch=ch):
                        gi = ch * 6 + i
                        cx.wait(te, "LSH", 32 * (gi + 1))
                        if i == 0:
                            cx.wait(te, "DLs", (ch + 1) * DT)
                        if gi >= 1:
                            cx.wait(te, "SHv", gi)
                        for ct in range(CT):
                            mm = te.matmul(out=psg[:], lhsT=wgu[(i % 2) * 2][:, ct * 128:(ct + 1) * 128],
                                           rhs=xnT[ct][:, tsl], start=(ct == 0), stop=(ct == CT - 1))
                        cx.inc("SHm", mm)
                        for ct in range(CT):
                            mm = te.matmul(out=psu[:], lhsT=wgu[(i % 2) * 2 + 1][:, ct * 128:(ct + 1) * 128],
                                           rhs=xnT[ct][:, tsl], start=(ct == 0), stop=(ct == CT - 1))
                        cx.inc("SHm", mm)

                    @block.scalar
                    def _(sc, i=i, ch=ch):
                        gi = ch * 6 + i
                        cx.wait(sc, "SHm", 2 * gi + 1)
                        if gi >= 2:
                            cx.wait(sc, "SHv", gi - 1)
                        cx.inc("SHs", sc.activation(out=shsc[i % 2][:], in_=psg[:], func=AF.Silu))

                    @block.vector
                    def _(ve, i=i, ch=ch):
                        cx.wait(ve, "SHs", ch * 6 + i + 1)
                        cx.wait(ve, "SHm", 2 * (ch * 6 + i) + 2)
                        if ch > 0 and i == 0:
                            cx.wait(ve, "WOm2", ch * 6)
                        cx.inc("SHv", ve.tensor_tensor(out=shT[i][:], in0=shsc[i % 2][:], in1=psu[:], op=ALU.mult))

            if stage < 10:
                continue
            with ExitStack() as pes:
                pr = [pes.enter_context(nc.psum_tensor(f"pr{ch}_{j}", [128, 512], F32)) for j in range(8)]
                for dt in range(DT):
                    @block.gpsimd
                    def _(gp, dt=dt, ch=ch):
                        gi = ch * DT + dt
                        if gi >= 2:
                            cx.wait(gp, "WOm", gi - 1)
                        cx.inc("LWO", gp.dma_start(out=wob[dt % 2][:], in_=W_out[dt * 128:(dt + 1) * 128, :]), 16)

                    @block.tensor
                    def _(te, dt=dt, ch=ch):
                        gi = ch * DT + dt
                        cx.wait(te, "LWO", 16 * (gi + 1))
                        if dt == 0:
                            cx.wait(te, "GAv", (ch + 1) * DT)
                            cx.wait(te, "SHv", (ch + 1) * 6)
                        for ts in range(NTS):
                            for h2 in range(2):
                                mm = te.matmul(out=pr[ts * 2 + h2][:],
                                               lhsT=dlu_t[dt][:, ts * 128:(ts + 1) * 128],
                                               rhs=wob[dt % 2][:, h2 * 512:(h2 + 1) * 512],
                                               start=(dt == 0), stop=False)
                        cx.inc("WOm", mm)

                for f in range(6):
                    @block.gpsimd
                    def _(gp, f=f, ch=ch):
                        gi = ch * 6 + f
                        if gi >= 2:
                            cx.wait(gp, "WOm2", gi - 1)
                        cx.inc("LSD", gp.dma_start(out=wsd[f % 2][:], in_=ws_down[f * 128:(f + 1) * 128, :]), 16)

                    @block.tensor
                    def _(te, f=f, ch=ch):
                        gi = ch * 6 + f
                        cx.wait(te, "LSD", 16 * (gi + 1))
                        if f == 0:
                            cx.wait(te, "SHv", (ch + 1) * 6)
                        for ts in range(NTS):
                            for h2 in range(2):
                                mm = te.matmul(out=pr[ts * 2 + h2][:],
                                               lhsT=shT[f][:, ts * 128:(ts + 1) * 128],
                                               rhs=wsd[f % 2][:, h2 * 512:(h2 + 1) * 512],
                                               start=False, stop=(f == 5))
                        cx.inc("WOm2", mm)

                for ts in range(NTS):
                    @block.sync
                    def _(sy, ts=ts, ch=ch):
                        gi = ch * NTS + ts
                        if gi >= 2:
                            cx.wait(sy, "STR1", 16 * (gi - 1))
                        cx.inc("LX2", sy.dma_start(out=r1x[ts % 2][:],
                                                   in_=x[t0 + ts * 128:t0 + (ts + 1) * 128, :]), 16)

                    @block.vector
                    def _(ve, ts=ts, ch=ch):
                        gi = ch * NTS + ts
                        cx.wait(ve, "WOm2", (ch + 1) * 6)
                        cx.wait(ve, "LX2", 16 * (gi + 1))
                        for h2 in range(2):
                            mm = ve.tensor_tensor(out=r1x[ts % 2][:, h2 * 512:(h2 + 1) * 512],
                                                  in0=r1x[ts % 2][:, h2 * 512:(h2 + 1) * 512],
                                                  in1=pr[ts * 2 + h2][:], op=ALU.add)
                        cx.inc("R1v", mm)

                    @block.sync
                    def _(sy, ts=ts, ch=ch):
                        cx.wait(sy, "R1v", ch * NTS + ts + 1)
                        cx.inc("STR1", sy.dma_start(out=r1[t0 + ts * 128:t0 + (ts + 1) * 128, :],
                                                    in_=r1x[ts % 2][:]), 16)


        @block.sync
        def _(sy):
            cx.wait(sy, "STXN", cx.val("STXN"))
            cx.wait(sy, "STR", cx.val("STR"))
            cx.wait(sy, "STR1", cx.val("STR1"))
            cx.wait(sy, "STBC", cx.val("STBC"))
    return nc


def host_prep_B(core, cnt_all, sel_c, eids_c, wts_c):
    cm = np.zeros((8, 1), np.float32)
    cm[:core] = 1.0
    tri = np.tril(np.ones((128, 128), np.float32), -1).T.astype(BF)
    tok = (core * T + np.arange(T).reshape(TT, 128).T).astype(np.float32)
    return {
        "cnt_all": np.ascontiguousarray(cnt_all.astype(np.float32)),
        "selB": np.ascontiguousarray(sel_c),
        "eidsB": np.ascontiguousarray(eids_c.astype(np.float32)),
        "wtsB": np.ascontiguousarray(wts_c.astype(np.float32)),
        "coremask": cm,
        "triB": np.ascontiguousarray(tri),
        "tokid": np.ascontiguousarray(np.repeat(tok[:, :, None], 2, axis=2).reshape(128, TT * 2)),
        "io64B": np.broadcast_to(np.arange(64, dtype=np.float32), (128, 64)).copy(),
        "onesB": np.ones((128, 1), BF),
    }


def build_B():
    nc = bass.Bass("TRN2", target_bir_lowering=False)
    D = nc.dram_tensor
    cnt_all = D("cnt_all", [8, NE], F32, kind="ExternalInput").ap()
    selB = D("selB", [T, NE], BF16, kind="ExternalInput").ap()
    eidsB = D("eidsB", [128, TT * TOPK], F32, kind="ExternalInput").ap()
    wtsB = D("wtsB", [128, TT * TOPK], F32, kind="ExternalInput").ap()
    coremask = D("coremask", [8, 1], F32, kind="ExternalInput").ap()
    triB = D("triB", [128, 128], BF16, kind="ExternalInput").ap()
    tokid = D("tokid", [128, TT * 2], F32, kind="ExternalInput").ap()
    io64B = D("io64B", [128, NE], F32, kind="ExternalInput").ap()
    onesB = D("onesB", [128, 1], BF16, kind="ExternalInput").ap()

    table = D("table", [NE * CAP + 1, 2], F32, kind="ExternalOutput").ap()
    islot = D("islot", [128, TT * TOPK], F32, kind="ExternalOutput").ap()
    wfin = D("wfin", [128, TT * TOPK], F32, kind="ExternalOutput").ap()
    base_s = D("base_s", [NE, 1], F32, kind="Internal").ap()

    es = ExitStack()
    with es:
        cx = Cx(nc, es)
        sbt = lambda n, s, d: es.enter_context(nc.sbuf_tensor(n, s, d))
        block = es.enter_context(nc.Block())
        s_cnt = sbt("s_cnt", [8, NE], F32)
        s_cm = sbt("s_cm", [8, 1], F32)
        s_sel = [sbt(f"s_sel{i}", [128, NE], BF16) for i in range(TT)]
        s_eids = sbt("s_eids", [128, TT * TOPK], F32)
        s_wts = sbt("s_wts", [128, TT * TOPK], F32)
        s_tri = sbt("s_tri", [128, 128], BF16)
        s_tok = sbt("s_tok", [128, TT * 2], F32)
        s_io = sbt("s_io", [128, NE], F32)
        s_ones = sbt("s_ones", [128, 1], BF16)
        s_base = sbt("s_base", [NE, 1], F32)
        s_bb = sbt("s_bb", [128, NE], F32)
        s_pos = sbt("s_pos", [128, NE], F32)
        s_ohk = sbt("s_ohk", [128, NE], F32)
        s_jk = sbt("s_jk", [128, NE], F32)
        s_col = sbt("s_col", [128, 8], F32)
        s_islot = sbt("s_islot", [128, TT * TOPK], F32)
        s_wfin = sbt("s_wfin", [128, TT * TOPK], F32)
        s_isc = sbt("s_isc", [128, TOPK], I32)

        @block.sync
        def _(sy):
            for apo, api in [(s_cnt[:], cnt_all), (s_eids[:], eidsB), (s_wts[:], wtsB),
                             (s_tri[:], triB), (s_tok[:], tokid), (s_io[:], io64B),
                             (s_ones[:], onesB)]:
                cx.inc("LD", sy.dma_start(out=apo, in_=api), 16)
            for tt in range(TT):
                cx.inc("LD", sy.dma_start(out=s_sel[tt][:], in_=selB[tt * 128:(tt + 1) * 128, :]), 16)

        breg = [None]

        @block.gpsimd
        def _(gp):
            cx.inc("LDG", gp.dma_start(out=s_cm[:], in_=coremask), 16)
            breg[0] = gp.to_reg(NE * CAP - 1)
        LD0 = cx.val("LD")

        with nc.psum_tensor("pb0", [128, 512], F32) as pb0, \
             nc.psum_tensor("pb1", [128, 512], F32) as pb1:
            @block.tensor
            def _(te):
                cx.wait(te, "LD", LD0)
                cx.wait(te, "LDG", 16)
                cx.inc("OFm", te.matmul(out=pb1[0:NE, 0:1], lhsT=s_cnt[:], rhs=s_cm[:],
                                        start=True, stop=True))

            @block.vector
            def _(ve):
                cx.wait(ve, "OFm", 1)
                cx.inc("BUP", ve.tensor_copy(out=s_base[:], in_=pb1[0:NE, 0:1]))

            for tt in range(TT):
                @block.sync
                def _(sy, tt=tt):
                    cx.wait(sy, "BUP", tt + 1)
                    cx.inc("STB", sy.dma_start(out=base_s, in_=s_base[:]), 16)
                    cx.wait(sy, "STB", 16 * (tt + 1))
                    cx.inc("LBB", sy.dma_start(
                        out=s_bb[:], in_=base_s.rearrange("e a -> a e").broadcast_to([128, NE])), 16)

                @block.tensor
                def _(te, tt=tt):
                    if tt > 0:
                        cx.wait(te, "POSe", tt)
                        cx.wait(te, "BUP", tt + 1)
                    cx.inc("POSm", te.matmul(out=pb0[:, 0:NE], lhsT=s_tri[:], rhs=s_sel[tt][:],
                                             start=True, stop=True))
                    cx.inc("CSm", te.matmul(out=pb1[0:NE, 0:1], lhsT=s_sel[tt][:], rhs=s_ones[:],
                                            start=True, stop=True))

                @block.vector
                def _(ve, tt=tt):
                    cx.wait(ve, "POSm", tt + 1)
                    cx.inc("POSe", ve.tensor_copy(out=s_pos[:], in_=pb0[:, 0:NE]))
                    cx.wait(ve, "LBB", 16 * (tt + 1))
                    BIGI = 1.0e6
                    for k in range(TOPK):
                        kc = tt * TOPK + k
                        ec = s_eids[:, kc:kc + 1]
                        cx.bump(ve, "v", ve.tensor_scalar(out=s_ohk[:], in0=s_io[:], scalar1=ec,
                                                          scalar2=None, op0=ALU.is_equal))
                        ve.tensor_tensor(out=s_jk[:], in0=s_pos[:], in1=s_ohk[:], op=ALU.mult)
                        cx.bump(ve, "v", ve.reduce_sum(out=s_col[:, 0:1], in_=s_jk[:], axis=AX.X))
                        ve.tensor_tensor(out=s_jk[:], in0=s_bb[:], in1=s_ohk[:], op=ALU.mult)
                        cx.bump(ve, "v", ve.reduce_sum(out=s_col[:, 1:2], in_=s_jk[:], axis=AX.X))
                        cx.bump(ve, "v", ve.tensor_tensor(out=s_col[:, 2:3], in0=s_col[:, 0:1],
                                                          in1=s_col[:, 1:2], op=ALU.add))
                        cx.bump(ve, "v", ve.tensor_scalar(out=s_col[:, 3:4], in0=s_col[:, 2:3],
                                                          scalar1=float(CAP), scalar2=None, op0=ALU.is_lt))
                        ve.tensor_tensor(out=s_wfin[:, kc:kc + 1], in0=s_wts[:, kc:kc + 1],
                                         in1=s_col[:, 3:4], op=ALU.mult)
                        cx.bump(ve, "v", ve.scalar_tensor_tensor(out=s_col[:, 4:5], in0=s_eids[:, kc:kc + 1],
                                                                 scalar=float(CAP), in1=s_col[:, 2:3],
                                                                 op0=ALU.mult, op1=ALU.add))
                        ve.tensor_tensor(out=s_islot[:, kc:kc + 1], in0=s_col[:, 4:5],
                                         in1=s_col[:, 3:4], op=ALU.mult)
                        cx.bump(ve, "v", ve.scalar_tensor_tensor(out=s_col[:, 5:6], in0=s_col[:, 3:4],
                                                                 scalar=-BIGI, in1=s_col[:, 4:5],
                                                                 op0=ALU.mult, op1=ALU.add))
                        cx.bump(ve, "v", ve.tensor_scalar_add(out=s_col[:, 5:6], in0=s_col[:, 5:6], scalar1=BIGI))
                        cx.bump(ve, "v", ve.tensor_copy(out=s_isc[:, k:k + 1], in_=s_col[:, 5:6]))
                    cx.inc("Kdone", ve.engine_nop())
                    cx.wait(ve, "CSm", tt + 1)
                    cx.bump(ve, "v", ve.tensor_tensor(out=s_base[:], in0=s_base[:],
                                                      in1=pb1[0:NE, 0:1], op=ALU.add))
                    cx.inc("BUP", ve.engine_nop())

                @block.gpsimd
                def _(gp, tt=tt):
                    cx.wait(gp, "Kdone", tt + 1)
                    if tt >= 2:
                        cx.wait(gp, "SCT", 16 * TOPK * (tt - 1))
                    for k in range(TOPK):
                        cx.inc("SCT", gp.indirect_dma_start(
                            out=table, out_offset=IndirectOffsetOnAxis(ap=s_isc[:, k:k + 1], axis=0),
                            in_=s_tok[:, 2 * tt:2 * tt + 2], in_offset=None,
                            bounds_check=breg[0], oob_is_err=False), 16)

        @block.sync
        def _(sy):
            cx.wait(sy, "Kdone", TT)
            cx.inc("STO", sy.dma_start(out=islot, in_=s_islot[:]), 16)
            cx.inc("STO", sy.dma_start(out=wfin, in_=s_wfin[:]), 16)
            cx.wait(sy, "STO", 32)
            cx.wait(sy, "SCT", cx.val("SCT"))
    return nc


def host_prep_C(inputs, core, xn_all, tables_stack):
    wg = np.asarray(inputs["L0_ffn_w_gate"], np.float32)[8 * core:8 * core + 8]
    wu = np.asarray(inputs["L0_ffn_w_up"], np.float32)[8 * core:8 * core + 8]
    wd = np.asarray(inputs["L0_ffn_w_down"], np.float32)[8 * core:8 * core + 8]
    wg_img = np.stack([colblocks(wg[e], 6) for e in range(8)])
    wu_img = np.stack([colblocks(wu[e], 6) for e in range(8)])
    return {
        "xn_all": np.ascontiguousarray(xn_all),
        "tablesC": np.ascontiguousarray(tables_stack),
        "wg_img": np.ascontiguousarray(wg_img),
        "wu_img": np.ascontiguousarray(wu_img),
        "wd_img": np.ascontiguousarray(wd),
        "idnC": np.eye(128, dtype=BF),
    }


def build_C():
    nc = bass.Bass("TRN2", target_bir_lowering=False)
    D = nc.dram_tensor
    xn_all = D("xn_all", [NCORE * T, C], BF16, kind="ExternalInput").ap()
    tablesC = D("tablesC", [8, 128, 32], F32, kind="ExternalInput").ap()
    wg_img = D("wg_img", [8, 6, 128, C], F32, kind="ExternalInput").ap()
    wu_img = D("wu_img", [8, 6, 128, C], F32, kind="ExternalInput").ap()
    wd_img = D("wd_img", [8, SF, C], F32, kind="ExternalInput").ap()
    idnC = D("idnC", [128, 128], BF16, kind="ExternalInput").ap()
    y_part = D("y_part", [8 * CAP, C], BF16, kind="ExternalOutput").ap()

    es = ExitStack()
    with es:
        cx = Cx(nc, es)
        sbt = lambda n, s, d: es.enter_context(nc.sbuf_tensor(n, s, d))
        block = es.enter_context(nc.Block())
        s_id = sbt("s_id", [128, 128], BF16)
        s_tball = sbt("s_tball", [128, 8 * 32], F32)
        s_mg = sbt("s_mg", [128, 32], F32)
        s_ti = sbt("s_ti", [128, 32], I32)
        xg = [sbt(f"xg{i}", [128, C], BF16) for i in range(2)]
        xgT = [sbt(f"xgT{i}", [128, CAP], BF16) for i in range(CT)]
        hT = [sbt(f"hT{i}", [128, CAP], BF16) for i in range(6)]
        wgb = [sbt(f"wgb{i}", [128, C], BF16) for i in range(2)]
        wub = [sbt(f"wub{i}", [128, C], BF16) for i in range(2)]
        wdb = [sbt(f"wdb{i}", [128, C], BF16) for i in range(2)]
        yev = [sbt(f"yev{i}", [128, C], BF16) for i in range(2)]
        sil = [sbt(f"sil{i}", [128, CAP], BF16) for i in range(2)]

        @block.sync
        def _(sy):
            cx.inc("LD", sy.dma_start(out=s_id[:], in_=idnC), 16)
            for c in range(8):
                cx.inc("LD", sy.dma_start(out=s_tball[:, c * 32:(c + 1) * 32], in_=tablesC[c]), 16)
        LD0 = cx.val("LD")

        @block.vector
        def _(ve):
            cx.wait(ve, "LD", LD0)
            cx.bump(ve, "v", ve.tensor_copy(out=s_mg[:], in_=s_tball[:, 0:32]))
            for c in range(1, 8):
                cx.bump(ve, "v", ve.tensor_tensor(out=s_mg[:], in0=s_mg[:],
                                                  in1=s_tball[:, c * 32:(c + 1) * 32], op=ALU.add))
            cx.bump(ve, "v", ve.tensor_copy(out=s_ti[:], in_=s_mg[:]))
            cx.inc("TI", ve.engine_nop())

        with nc.psum_tensor("ptpA", [128, 1024], BF16) as ptpA, \
             nc.psum_tensor("ptpB", [128, 1024], BF16) as ptpB, \
             nc.psum_tensor("pgC", [128, 512], F32) as pgC, \
             nc.psum_tensor("puC", [128, 512], F32) as puC, \
             nc.psum_tensor("pd0", [128, 512], F32) as pd0, \
             nc.psum_tensor("pd1", [128, 512], F32) as pd1:
            ptp2 = [ptpA, ptpB]
            for e in range(8):
                for g in range(4):
                    gg = e * 4 + g
                    @block.gpsimd
                    def _(gp, e=e, g=g, gg=gg):
                        if gg == 0:
                            cx.wait(gp, "TI", 1)
                        if gg >= 2:
                            cx.wait(gp, "TPm", gg - 1)
                        cx.inc("GA", gp.indirect_dma_start(
                            out=xg[g % 2][:], out_offset=None, in_=xn_all,
                            in_offset=IndirectOffsetOnAxis(ap=s_ti[:, gg:gg + 1], axis=0)), 16)

                    @block.tensor
                    def _(te, e=e, g=g, gg=gg):
                        cx.wait(te, "GA", 16 * (gg + 1))
                        if gg >= 2:
                            cx.wait(te, "TPe", (gg - 1) * CT)
                        for ct in range(CT):
                            mm = te.transpose(out=ptp2[g % 2][:, ct * 128:(ct + 1) * 128],
                                              in_=xg[g % 2][:, ct * 128:(ct + 1) * 128],
                                              identity=s_id[:])
                        cx.inc("TPm", mm)

                    @block.vector
                    def _(ve, e=e, g=g, gg=gg):
                        cx.wait(ve, "TPm", gg + 1)
                        for ct in range(CT):
                            mm = ve.tensor_copy(out=xgT[ct][:, g * 128:(g + 1) * 128],
                                                in_=ptp2[g % 2][:, ct * 128:(ct + 1) * 128])
                        cx.inc("TPe", mm, CT)

                for f in range(6):
                    fg = e * 6 + f
                    @block.gpsimd
                    def _(gp, e=e, f=f, fg=fg):
                        if fg >= 2:
                            cx.wait(gp, "GUm", 2 * (fg - 1))
                        cx.inc("LGU", gp.dma_start(out=wgb[f % 2][:], in_=wg_img[e, f]), 16)
                        cx.inc("LGU", gp.dma_start(out=wub[f % 2][:], in_=wu_img[e, f]), 16)

                    @block.tensor
                    def _(te, e=e, f=f, fg=fg):
                        cx.wait(te, "LGU", 32 * (fg + 1))
                        if f == 0:
                            cx.wait(te, "TPe", (e * 4 + 4) * CT)
                        if fg >= 1:
                            cx.wait(te, "SIs", fg)
                            cx.wait(te, "GUe", fg)
                        for ct in range(CT):
                            mm = te.matmul(out=pgC[:], lhsT=wgb[f % 2][:, ct * 128:(ct + 1) * 128],
                                           rhs=xgT[ct][:], start=(ct == 0), stop=(ct == CT - 1))
                        cx.inc("GUm", mm)
                        for ct in range(CT):
                            mm = te.matmul(out=puC[:], lhsT=wub[f % 2][:, ct * 128:(ct + 1) * 128],
                                           rhs=xgT[ct][:], start=(ct == 0), stop=(ct == CT - 1))
                        cx.inc("GUm", mm)

                    @block.scalar
                    def _(sc, e=e, f=f, fg=fg):
                        cx.wait(sc, "GUm", 2 * fg + 1)
                        if fg >= 2:
                            cx.wait(sc, "GUe", fg - 1)
                        cx.inc("SIs", sc.activation(out=sil[f % 2][:], in_=pgC[:], func=AF.Silu))

                    @block.vector
                    def _(ve, e=e, f=f, fg=fg):
                        cx.wait(ve, "SIs", fg + 1)
                        cx.wait(ve, "GUm", 2 * fg + 2)
                        if f == 0 and e > 0:
                            cx.wait(ve, "DNmf", 24 * e)
                        cx.inc("GUe", ve.tensor_tensor(out=hT[f][:], in0=sil[f % 2][:], in1=puC[:],
                                                       op=ALU.mult))

                for st in range(4):
                    gst = e * 4 + st
                    @block.gpsimd
                    def _(gp, e=e, st=st, gst=gst):
                        for f in range(6):
                            L = gst * 6 + f
                            if L >= 2:
                                cx.wait(gp, "DNmf", L - 1)
                            cx.inc("LDN", gp.dma_start(out=wdb[f % 2][:],
                                                       in_=wd_img[e, f * 128:(f + 1) * 128, :]), 16)

                    @block.tensor
                    def _(te, e=e, st=st, gst=gst):
                        if st == 0:
                            cx.wait(te, "GUe", (e + 1) * 6)
                        if gst >= 1:
                            cx.wait(te, "DNe", gst)
                        for f in range(6):
                            L = gst * 6 + f
                            cx.wait(te, "LDN", 16 * (L + 1))
                            te.matmul(out=pd0[:], lhsT=hT[f][:, st * 128:(st + 1) * 128],
                                      rhs=wdb[f % 2][:, 0:512], start=(f == 0), stop=(f == 5))
                            mm = te.matmul(out=pd1[:], lhsT=hT[f][:, st * 128:(st + 1) * 128],
                                           rhs=wdb[f % 2][:, 512:1024], start=(f == 0), stop=(f == 5))
                            cx.inc("DNmf", mm)

                    @block.vector
                    def _(ve, e=e, st=st, gst=gst):
                        cx.wait(ve, "DNmf", (gst + 1) * 6)
                        if gst >= 2:
                            cx.wait(ve, "STY", 16 * (gst - 1))
                        ve.tensor_copy(out=yev[st % 2][:, 0:512], in_=pd0[:])
                        mm = ve.tensor_copy(out=yev[st % 2][:, 512:1024], in_=pd1[:])
                        cx.inc("DNe", mm)

                    @block.sync
                    def _(sy, e=e, st=st, gst=gst):
                        cx.wait(sy, "DNe", gst + 1)
                        cx.inc("STY", sy.dma_start(
                            out=y_part[(e * 4 + st) * 128:(e * 4 + st + 1) * 128, :],
                            in_=yev[st % 2][:]), 16)

        @block.sync
        def _(sy):
            cx.wait(sy, "STY", 16 * 32)
    return nc


HK, HV, DK, DV = 4, 8, 128, 128
KEY = HK * DK
VAL = HV * DV
CONVD = 2 * KEY + VAL
NEX, INTER = 8, 512
HD, DD = 16, 64
ROPE, NOPE = 32, 32
RD = 128


def host_prep_D(inputs, core, r1, y_all, islot, wfin):
    ii = lambda k: np.asarray(inputs[k], np.float32)
    perm = []
    for h in range(HD):
        perm += list(range(64 * h, 64 * h + 32))
    ev, od = [], []
    for h in range(HD):
        for i in range(16):
            ev.append(64 * h + 32 + 2 * i)
            od.append(64 * h + 32 + 2 * i + 1)
    perm = perm + ev + od
    Wq = ii("L1_ffn_Wq")[:, perm]
    wkva = ii("L1_ffn_Wkva")
    wlat = wkva[:, 0:RD]
    wkre = wkva[:, RD + 0:RD + 32:2]
    wkro = wkva[:, RD + 1:RD + 32:2]
    Wo = ii("L1_ffn_Wo")
    rows = []
    for h in range(HD):
        rows += list(range(64 * h, 64 * h + 32))
    Wo_sub = Wo[rows, :]
    t = np.arange(T, dtype=np.float64)
    fr = 1.0 / (1e5 ** (np.arange(0, DD // 2, 2, dtype=np.float64) / (DD // 2)))
    ang = np.outer(t, fr)
    cosb = np.tile(np.cos(ang).T.astype(np.float32), (8, 1))
    sinb = np.tile(np.sin(ang).T.astype(np.float32), (8, 1))
    strict = np.tril(np.ones((128, 128), np.float32), -1).astype(BF)
    incl = np.tril(np.ones((128, 128), np.float32), 0).astype(BF)
    cmask = np.triu(np.ones((128, 128), np.float32), 0).astype(BF)
    wge = np.asarray(inputs["L1_attn_W_gate_e"], np.float32)
    wue = np.asarray(inputs["L1_attn_W_up_e"], np.float32)
    wde = np.asarray(inputs["L1_attn_W_down_e"], np.float32)
    wgf = np.asarray(inputs["L1_ffn_Wg"], np.float32)
    wuf = np.asarray(inputs["L1_ffn_Wu"], np.float32)
    wdf = np.asarray(inputs["L1_ffn_Wd"], np.float32)
    return {
        "r1D": np.ascontiguousarray(r1),
        "y_allD": np.ascontiguousarray(y_all),
        "islotD": np.ascontiguousarray(islot.astype(np.float32)),
        "wfinD": np.ascontiguousarray(wfin.astype(np.float32)),
        "idnbD": np.eye(128, dtype=BF),
        "idnfD": np.eye(128, dtype=np.float32),
        "w_ln1a": vec2d(ii("L1_attn_w_ln1")),
        "w_ln2a": vec2d(1.0 + ii("L1_attn_w_ln2")),
        "w_ln1f": vec2d(ii("L1_ffn_w_ln1")),
        "w_ln2f": vec2d(ii("L1_ffn_w_ln2")),
        "w_ln1a_row": ii("L1_attn_w_ln1").reshape(1, C),
        "w_ln1f_row": ii("L1_ffn_w_ln1").reshape(1, C),
        "wqkvz": colblocks(ii("L1_attn_W_qkvz"), 24),
        "wbab": colblocks16(ii("L1_attn_W_ba")[:, 0:8]),
        "wbaa": colblocks16(ii("L1_attn_W_ba")[:, 8:16]),
        "convwD": np.ascontiguousarray(ii("L1_attn_conv_w").reshape(16, 128, 4).transpose(1, 0, 2).reshape(128, 64)),
        "dtb": ii("L1_attn_dt_bias").reshape(8, 1),
        "alogD": ii("L1_attn_A_log").reshape(8, 1),
        "gnw": ii("L1_attn_gate_norm_w").reshape(1, 128),
        "woutD": colblocks(ii("L1_attn_W_out"), 8),
        "wrtr": colblocks16(ii("L1_attn_W_router")),
        "wge": np.stack([colblocks(wge[e], 4) for e in range(8)]),
        "wue": np.stack([colblocks(wue[e], 4) for e in range(8)]),
        "wde": np.ascontiguousarray(wde),
        "wqD": colblocks(Wq, 8),
        "wlat": colblocks(wlat, 1),
        "wkre": np.ascontiguousarray(colblocks16(wkre)),
        "wkro": np.ascontiguousarray(colblocks16(wkro)),
        "wkvb": ii("L1_ffn_Wkvb"),
        "woD": colblocks(Wo_sub, 8),
        "cosb": cosb.astype(BF), "sinb": sinb.astype(BF),
        "strict": strict, "incl": incl, "cmask": cmask,
        "cmask4": np.ascontiguousarray(np.concatenate(
            [(np.arange(128)[:, None] + 128 * r <= np.arange(512)[None, :]).astype(np.float32).astype(BF)
             for r in range(4)], axis=1)),
        "wgate": np.ascontiguousarray(colblocks16(ii("L1_ffn_Wgate")[:, 0:8])),
        "ebias": ii("L1_ffn_expert_bias").reshape(8, 1),
        "wgf": np.stack([colblocks(wgf[e], 4) for e in range(8)]),
        "wuf": np.stack([colblocks(wuf[e], 4) for e in range(8)]),
        "wdf": np.ascontiguousarray(wdf),
        "wsg": colblocks(np.asarray(inputs["L1_ffn_sWg"], np.float32)[0], 4),
        "wsu": colblocks(np.asarray(inputs["L1_ffn_sWu"], np.float32)[0], 4),
        "wsd": np.ascontiguousarray(np.asarray(inputs["L1_ffn_sWd"], np.float32)[0]),
    }


def colblocks16(w):
    R, m = w.shape
    return np.ascontiguousarray(np.asarray(w, np.float32).reshape(R // 128, 128, m)
                                .transpose(1, 0, 2).reshape(128, (R // 128) * m))


def colblocks_router(w):
    return colblocks16(w)

def build_D(stage=None, debug=False):
    nc = bass.Bass("TRN2", target_bir_lowering=False)
    D = nc.dram_tensor
    r1D = D("r1D", [T, C], F32, kind="ExternalInput").ap()
    y_allD = D("y_allD", [NE * CAP, C], BF16, kind="ExternalInput").ap()
    islotD = D("islotD", [128, TT * TOPK], F32, kind="ExternalInput").ap()
    wfinD = D("wfinD", [128, TT * TOPK], F32, kind="ExternalInput").ap()
    idnbD = D("idnbD", [128, 128], BF16, kind="ExternalInput").ap()
    idnfD = D("idnfD", [128, 128], F32, kind="ExternalInput").ap()
    w_ln1a = D("w_ln1a", [128, CT], F32, kind="ExternalInput").ap()
    w_ln2a = D("w_ln2a", [128, CT], F32, kind="ExternalInput").ap()
    w_ln1f = D("w_ln1f", [128, CT], F32, kind="ExternalInput").ap()
    w_ln2f = D("w_ln2f", [128, CT], F32, kind="ExternalInput").ap()
    wqkvz = D("wqkvz", [24, 128, C], F32, kind="ExternalInput").ap()
    wbab = D("wbab", [128, CT * 8], F32, kind="ExternalInput").ap()
    wbaa = D("wbaa", [128, CT * 8], F32, kind="ExternalInput").ap()
    convwD = D("convwD", [128, 64], F32, kind="ExternalInput").ap()
    dtb = D("dtb", [8, 1], F32, kind="ExternalInput").ap()
    alogD = D("alogD", [8, 1], F32, kind="ExternalInput").ap()
    gnw = D("gnw", [1, 128], F32, kind="ExternalInput").ap()
    woutD = D("woutD", [8, 128, C], F32, kind="ExternalInput").ap()
    wrtr = D("wrtr", [128, CT * 8], F32, kind="ExternalInput").ap()
    wge = D("wge", [8, 4, 128, C], F32, kind="ExternalInput").ap()
    wue = D("wue", [8, 4, 128, C], F32, kind="ExternalInput").ap()
    wde = D("wde", [8, INTER, C], F32, kind="ExternalInput").ap()
    wqD = D("wqD", [8, 128, C], F32, kind="ExternalInput").ap()
    wlat = D("wlat", [1, 128, C], F32, kind="ExternalInput").ap()
    wkre = D("wkre", [128, CT * 16], F32, kind="ExternalInput").ap()
    wkro = D("wkro", [128, CT * 16], F32, kind="ExternalInput").ap()
    wkvb = D("wkvb", [128, 512], F32, kind="ExternalInput").ap()
    woD = D("woD", [8, 128, 512], F32, kind="ExternalInput").ap()
    cosb = D("cosb", [128, T], BF16, kind="ExternalInput").ap()
    sinb = D("sinb", [128, T], BF16, kind="ExternalInput").ap()
    strictD = D("strict", [128, 128], BF16, kind="ExternalInput").ap()
    inclD = D("incl", [128, 128], BF16, kind="ExternalInput").ap()
    cmaskD = D("cmask", [128, 128], BF16, kind="ExternalInput").ap()
    cmask4 = D("cmask4", [128, 4 * 512], BF16, kind="ExternalInput").ap()
    wgate = D("wgate", [128, CT * 8], F32, kind="ExternalInput").ap()
    ebias = D("ebias", [8, 1], F32, kind="ExternalInput").ap()
    wgf = D("wgf", [8, 4, 128, C], F32, kind="ExternalInput").ap()
    wuf = D("wuf", [8, 4, 128, C], F32, kind="ExternalInput").ap()
    wdf = D("wdf", [8, INTER, C], F32, kind="ExternalInput").ap()
    wsg = D("wsg", [4, 128, C], F32, kind="ExternalInput").ap()
    wsu = D("wsu", [4, 128, C], F32, kind="ExternalInput").ap()
    wsd = D("wsd", [INTER, C], F32, kind="ExternalInput").ap()

    w_ln1a_row = D("w_ln1a_row", [1, C], F32, kind="ExternalInput").ap()
    w_ln1f_row = D("w_ln1f_row", [1, C], F32, kind="ExternalInput").ap()
    outD = D("outD", [T, C], F32, kind="ExternalOutput").ap()
    dk = "ExternalOutput" if debug else "Internal"
    ob_s = D("ob_s", [T, C], F32, kind=dk).ap()
    xft_s = D("xft_s", [C, T], BF16, kind=dk).ap()
    xnt_s = D("xnt_s", [C, T], BF16, kind=dk).ap()
    if debug:
        hT_dbg = D("hT_dbg", [8, 128, T], BF16, kind="ExternalOutput").ap()
        o5_dbg = D("o5_dbg", [8, 128, T], BF16, kind="ExternalOutput").ap()
        x2_dbg = D("x2_dbg", [8, 128, T], BF16, kind="ExternalOutput").ap()
        rt_dbg = D("rt_dbg", [128, 128], F32, kind="ExternalOutput").ap()
        he_dbg = D("he_dbg", [4, 128, T], BF16, kind="ExternalOutput").ap()
        lg_dbg = D("lg_dbg", [8, T], F32, kind="ExternalOutput").ap()
        yf_dbg = D("yf_dbg", [4, 128, T], BF16, kind="ExternalOutput").ap()
        xa_dbg = D("xa_dbg", [8, 128, T], BF16, kind="ExternalOutput").ap()
        o9_dbg = D("o9_dbg", [8, 128, T], BF16, kind="ExternalOutput").ap()

    es = ExitStack()
    with es:
        cx = VCx(nc, es, gnames=("GAD", "LD0g", "LKV", "LQF", "LQF2", "LQK",
                                 "LW4", "LWO2", "LBA", "m1Lg", "m1Ld", "m2Lg", "m2Ld", "DBG"))
        sbt = lambda n, s, d: es.enter_context(nc.sbuf_tensor(n, s, d))
        block = es.enter_context(nc.Block())

        def bar(engines=("sync", "vector", "scalar", "tensor", "gpsimd")):
            cx.new_epoch()
            tgt = cx.val("PH") + len(engines)
            cx.cnt["PH"] = tgt
            for e in engines:
                sec = getattr(block, e)
                def _f(eng, tgt=tgt):
                    eng.drain().then_inc(cx.sem("PH"), 1)
                    eng.wait_ge(cx.sem("PH"), tgt)
                sec(_f)

        def _dump(pairs):
            if not debug:
                return

            @block.sync
            def _(sy):
                for dst, src in pairs:
                    cx.inc("DBG", sy.dma_start(out=dst, in_=src), 16)
                cx.wait(sy, "DBG", cx.val("DBG"))

        def _cut(k):
            if stage is None or stage > k:
                return False

            @block.sync
            def _(sy):
                cx.inc("XCUT", sy.dma_start(out=outD[0:128, :], in_=scr2.bitcast(F32)[:, 0:1024]), 16)
                cx.wait(sy, "XCUT", 16)
            return True

        sid = sbt("sid", [128, 128], BF16)
        sidf = sbt("sidf", [128, 128], F32)
        smask = sbt("smask", [128, 2 * 128], BF16)
        arena = sbt("arena", [128, 9216], BF16)
        scr2 = sbt("scr2", [128, 6144], BF16)
        scf = scr2.bitcast(F32)
        r1t = [scf[:, 1024:2048], scf[:, 2048:3072]]
        acc1 = arena.bitcast(F32)[:, 0:1024]
        gkb = [arena[:, 2048 + i * 1024:2048 + (i + 1) * 1024] for i in range(2)]
        isl = sbt("isl", [128, TT * TOPK], I32)
        islf = arena.bitcast(F32)[:, 0:128]
        stat = sbt("stat", [128, 12], F32)
        sqf = arena[:, 4096:5120]
        x1n = [arena[:, 5120 + i * 1024:5120 + (i + 1) * 1024] for i in range(2)]
        xhat = arena[:, 7168:8192]
        xtld = arena[:, 8192:9216]
        wlnbc = scr2[:, 0:2048]
        xhT = [sbt(f"xhT{i}", [128, T], BF16) for i in range(CT)]
        qkvc = [sbt(f"qkvc{i}", [128, T], BF16) for i in range(16)]
        zT = [sbt(f"zT{i}", [128, T], BF16) for i in range(8)]
        craw = [sbt(f"craw{i}", [128, T + 4], BF16) for i in range(2)]
        cvsc = [sbt(f"cvsc{i}", [128, T], BF16) for i in range(2)]
        wblk = [sbt(f"wblk{i}", [128, C], BF16) for i in range(2)]
        wblk2 = [sbt(f"wblk2{i}", [128, C], BF16) for i in range(2)]
        wdq = [wblk[0], wblk[1], wblk2[0], wblk2[1]]
        cvw = sbt("cvw", [128, 64], F32)
        wbabS = sbt("wbabS", [128, CT * 8], BF16)
        wbaaS = sbt("wbaaS", [128, CT * 8], BF16)
        wrtS = sbt("wrtS", [128, CT * 8], BF16)
        wgtS = sbt("wgtS", [128, CT * 8], BF16)
        ebiasS = sbt("ebiasS", [8, 1], F32)
        rpackA = sbt("rpackA", [128, T], F32)
        wfn = rpackA[:, 0:128]
        rpackB = sbt("rpackB", [128, T], F32)
        rpackC = sbt("rpackC", [128, T], BF16)
        rows8 = rpackA[0:8, :]
        rowsA = arena.bitcast(F32)[0:8, 512:2560]
        beta8 = arena.bitcast(F32)[0:8, 2560:4608]
        crel8 = scf[0:8, 1024:3072]
        cend8 = rpackB[0:8, :]
        ones8 = sbt("ones8", [8, 128], F32)
        onesr = sbt("onesr", [1, 128], BF16)
        onesf = sbt("onesf", [1, 128], F32)
        onesc = sbt("onesc", [128, 1], BF16)
        dtbs = sbt("dtbs", [8, 1], F32)
        alg = sbt("alg", [8, 1], F32)
        bgcol = arena.bitcast(F32)[:, 0:384]
        gnwb = sbt("gnwb", [128, 128], F32)
        sqn = scr2[:, 0:512]
        nrow = scf[0:1, 512:1024]
        nrowb = scr2[0:1, 512:1024]
        dxp = scr2[:, 0:128]
        dg = scr2[:, 128:256]
        dn = scr2[:, 256:384]
        dprod = scr2[:, 384:512]
        dcur = scr2[:, 512:640]
        dtmp = scr2[:, 640:768]
        dtmp2 = scr2[:, 768:896]
        daq = scr2[:, 896:1024]
        dkg = scr2[:, 1024:1152]
        drt = scr2[:, 1152:1280]
        dde = scr2[:, 1280:1408]
        don = scr2[:, 1408:1536]
        dS = [scr2[:, 1536 + i * 128:1536 + (i + 1) * 128] for i in range(2 * HV)]
        semi = scf[:, 1792:1920]
        rowp = scr2[0:1, 3840:4352]
        rowpf = scf[0:1, 2176:2304]
        dcol = sbt("dcol", [128, 8], F32)
        ccol = sbt("ccol", [128, 6], F32)
        lgT = rpackB[0:8, :]
        lg8 = sbt("lg8", [128, 8], F32)
        oh8 = sbt("oh8", [128, 8], F32)
        oh8b = sbt("oh8b", [128, 8], F32)
        routeTM = scf[:, 1792:1920]
        latT = scr2[:, 0:2048]
        routeP = latT[0:1, :]
        rbcS = qkvc[15]
        hE = [qkvc[8 + i] for i in range(4)]
        krop = rpackA.bitcast(BF16)[0:64, 0:2048]
        krodd = rpackB.bitcast(BF16)[0:16, 0:2048]
        kvT = [cvsc[0][:, 0:T], cvsc[1][:, 0:T], craw[0][:, 0:T], craw[1][:, 0:T]]
        vaug = arena[:, 0:TT * 528]
        yeff = hE
        qpk = rpackA.bitcast(BF16)[0:64, 2048:4096]
        kpk = rpackC[0:64, :]
        pscr = [scr2[:, 2048:2560]]
        wkvbS = scr2[:, 2560:3072]
        rrow = scf[0:1, 0:512]
        rrowb = scr2[0:1, 1024:1536]
        cosS = qkvc[12]
        sinS = qkvc[13]
        obt = r1t
        yst = scr2[0:33, 3072:3584]
        rrow2 = scr2[0:1, 1536:2048]
        cm4 = qkvc[14]

        xtT = qkvc[0:8]
        xnT1v = qkvc[8:16]
        coreT = xhT
        hT = zT
        outT = [[qkvc[i][:, 0:1024], qkvc[i][:, 1024:2048]] for i in range(8)]
        qTf = zT
        xaT = xhT

        @block.sync
        def _(sy):
            for apo, api in [(sid[:], idnbD), (sidf[:], idnfD),
                             (smask[:, 0:128], strictD), (smask[:, 128:256], inclD),
                             (islf[:], islotD), (wfn[:], wfinD),
                             (dtbs[:], dtb), (alg[:], alogD),
                             (gnwb[:], gnw.broadcast_to([128, 128])),
                             (cvw[:], convwD), (ebiasS[:], ebias)]:
                cx.inc("LD0", sy.dma_start(out=apo, in_=api), 16)

        @block.gpsimd
        def _(gp):
            cx.inc("LD0g", gp.dma_start(out=wlnbc[:, 0:C], in_=w_ln1a_row.broadcast_to([128, C])), 16)
            cx.inc("LD0g", gp.dma_start(out=wlnbc[:, C:2 * C], in_=w_ln1f_row.broadcast_to([128, C])), 16)
            cx.inc("LD0g", gp.dma_start(out=wbabS[:], in_=wbab), 16)
            cx.inc("LD0g", gp.dma_start(out=wbaaS[:], in_=wbaa), 16)
            cx.inc("LD0g", gp.dma_start(out=wrtS[:], in_=wrtr), 16)
            cx.inc("LD0g", gp.dma_start(out=wgtS[:], in_=wgate), 16)
        LD0v = cx.val("LD0")
        LD0gv = cx.val("LD0g")

        @block.vector
        def _(ve):
            cx.wait(ve, "LD0", LD0v)
            cx.wait(ve, "LD0g", LD0gv)
            cx.bump(ve, "v", ve.tensor_scalar_add(out=wlnbc[:, 0:C], in0=wlnbc[:, 0:C], scalar1=1.0))
            cx.bump(ve, "v", ve.tensor_copy(out=isl[:], in_=islf[:]))
            ve.memset(ones8[:], 1.0)
            ve.memset(onesr[:], 1.0)
            ve.memset(onesf[:], 1.0)
            cx.bump(ve, "v", ve.memset(onesc[:], 1.0))
            cx.inc("ISL", ve.engine_nop())

        @block.scalar
        def _(sc):
            cx.wait(sc, "LD0", LD0v)
            sc.activation(out=alg[:], in_=alg[:], func=AF.Exp)
            cx.inc("ALG", sc.mul(out=alg[:], in_=alg[:], mul=-1.0))

        with nc.psum_tensor("ptpD0", [128, 1024], BF16) as ptpD0, \
             nc.psum_tensor("ptpD1", [128, 1024], BF16) as ptpD1:
            ptpD = [ptpD0, ptpD1]
            for tt in range(TT):
                pb = tt % 2
                @block.sync
                def _(sy, tt=tt, pb=pb):
                    if tt >= 2:
                        cx.wait(sy, "OB", 16 * (tt - 1))
                    cx.inc("LDR", sy.dma_start(out=r1t[pb][:], in_=r1D[tt * 128:(tt + 1) * 128, :]), 16)

                @block.gpsimd
                def _(gp, tt=tt):
                    if tt == 0:
                        cx.wait(gp, "ISL", 1)
                    for k in range(TOPK):
                        gk = tt * TOPK + k
                        if gk >= 2:
                            cx.wait(gp, "D1a", gk - 1)
                        cx.inc("GAD", gp.indirect_dma_start(
                            out=gkb[k % 2][:], out_offset=None, in_=y_allD,
                            in_offset=IndirectOffsetOnAxis(ap=isl[:, gk:gk + 1], axis=0)), 16)

                @block.vector
                def _(ve, tt=tt, pb=pb):
                    for k in range(TOPK):
                        gk = tt * TOPK + k
                        cx.wait(ve, "GAD", 16 * (gk + 1))
                        if k == 0:
                            mm = ve.tensor_scalar(out=acc1[:], in0=gkb[0][:], scalar1=wfn[:, gk:gk + 1],
                                                  scalar2=None, op0=ALU.mult)
                        else:
                            mm = ve.scalar_tensor_tensor(out=acc1[:], in0=gkb[k % 2][:],
                                                         scalar=wfn[:, gk:gk + 1], in1=acc1[:],
                                                         op0=ALU.mult, op1=ALU.add)
                        cx.inc("D1a", mm)
                    cx.wait(ve, "LDR", 16 * (tt + 1))
                    cx.inc("D1x", ve.tensor_tensor(out=acc1[:], in0=acc1[:], in1=r1t[pb][:], op=ALU.add))

                @block.scalar
                def _(sc, tt=tt):
                    cx.wait(sc, "D1x", tt + 1)
                    cx.inc("LNs", sc.activation(out=sqf[:], in_=acc1[:], func=AF.Square))

                @block.vector
                def _(ve, tt=tt):
                    cx.wait(ve, "LNs", tt + 1)
                    cx.bump(ve, "v", ve.reduce_sum(out=stat[:, 0:1], in_=acc1[:], axis=AX.X))
                    cx.bump(ve, "v", ve.reduce_sum(out=stat[:, 1:2], in_=sqf[:], axis=AX.X))
                    cx.bump(ve, "v", ve.tensor_scalar_mul(out=stat[:, 0:1], in0=stat[:, 0:1], scalar1=1.0 / C))
                    cx.bump(ve, "v", ve.tensor_scalar_mul(out=stat[:, 1:2], in0=stat[:, 1:2], scalar1=1.0 / C))
                    cx.bump(ve, "v", ve.tensor_tensor(out=stat[:, 8:9], in0=stat[:, 0:1], in1=stat[:, 0:1], op=ALU.mult))
                    cx.bump(ve, "v", ve.tensor_tensor(out=stat[:, 3:4], in0=stat[:, 1:2], in1=stat[:, 8:9], op=ALU.subtract))
                    cx.bump(ve, "v", ve.tensor_scalar_add(out=stat[:, 3:4], in0=stat[:, 3:4], scalar1=1e-5))
                    cx.inc("LNv", ve.engine_nop())

                @block.scalar
                def _(sc, tt=tt):
                    cx.wait(sc, "LNv", tt + 1)
                    cx.bump(sc, "s", sc.activation(out=stat[:, 4:5], in_=stat[:, 3:4], func=AF.Ln))
                    cx.bump(sc, "s", sc.activation(out=stat[:, 5:6], in_=stat[:, 4:5], func=AF.Exp, scale=-0.5))
                    cx.inc("LNr", sc.nop())

                @block.vector
                def _(ve, tt=tt):
                    cx.wait(ve, "LNr", tt + 1)
                    cx.bump(ve, "v", ve.tensor_tensor(out=stat[:, 6:7], in0=stat[:, 0:1], in1=stat[:, 5:6], op=ALU.mult))
                    cx.bump(ve, "v", ve.tensor_scalar_mul(out=stat[:, 6:7], in0=stat[:, 6:7], scalar1=-1.0))
                    cx.bump(ve, "v", ve.tensor_scalar_add(out=stat[:, 9:10], in0=stat[:, 3:4], scalar1=-1e-5))
                    cx.bump(ve, "v", ve.tensor_tensor(out=stat[:, 9:10], in0=stat[:, 9:10], in1=stat[:, 5:6], op=ALU.mult))
                    cx.bump(ve, "v", ve.tensor_tensor(out=stat[:, 9:10], in0=stat[:, 9:10], in1=stat[:, 5:6], op=ALU.mult))
                    cx.bump(ve, "v", ve.tensor_scalar_add(out=stat[:, 10:11], in0=stat[:, 9:10], scalar1=1e-6))
                    cx.bump(ve, "v", ve.tensor_scalar_add(out=stat[:, 11:12], in0=stat[:, 9:10], scalar1=1e-5))
                    cx.inc("LNb", ve.engine_nop())

                @block.scalar
                def _(sc, tt=tt, pb=pb):
                    cx.wait(sc, "LNb", tt + 1)
                    if tt >= 2:
                        cx.wait(sc, "TPD", (tt - 1) * 3 * CT)
                        cx.wait(sc, "OB", 16 * (tt - 1))
                    cx.bump(sc, "s", sc.activation(out=stat[:, 10:11], in_=stat[:, 10:11], func=AF.Ln))
                    cx.bump(sc, "s", sc.activation(out=stat[:, 10:11], in_=stat[:, 10:11], func=AF.Exp, scale=-0.5))
                    cx.bump(sc, "s", sc.activation(out=stat[:, 11:12], in_=stat[:, 11:12], func=AF.Ln))
                    cx.bump(sc, "s", sc.activation(out=stat[:, 11:12], in_=stat[:, 11:12], func=AF.Exp, scale=-0.5))
                    cx.inc("XN1", sc.activation(out=x1n[pb][:], in_=acc1[:], func=AF.Identity,
                                                bias=stat[:, 6:7], scale=stat[:, 5:6]))

                @block.vector
                def _(ve, tt=tt, pb=pb):
                    cx.wait(ve, "XN1", tt + 1)
                    cx.inc("OBv", ve.scalar_tensor_tensor(out=r1t[pb][:], in0=x1n[pb][:], scalar=2.0,
                                                          in1=acc1[:], op0=ALU.mult, op1=ALU.add))

                @block.sync
                def _(sy, tt=tt, pb=pb):
                    cx.wait(sy, "OBv", tt + 1)
                    cx.inc("OB", sy.dma_start(out=ob_s[tt * 128:(tt + 1) * 128, :], in_=r1t[pb][:]), 16)

                @block.scalar
                def _(sc, tt=tt, pb=pb):
                    cx.wait(sc, "XN1", tt + 1)
                    sc.activation(out=xhat[:], in_=x1n[pb][:], func=AF.Identity, scale=stat[:, 10:11])
                    cx.inc("ZRs", sc.activation(out=xtld[:], in_=x1n[pb][:], func=AF.Identity, scale=stat[:, 11:12]))

                @block.vector
                def _(ve, tt=tt):
                    cx.wait(ve, "ZRs", tt + 1)
                    ve.tensor_tensor(out=xhat[:], in0=xhat[:], in1=wlnbc[:, 0:C], op=ALU.mult)
                    cx.inc("ZRm", ve.tensor_tensor(out=xtld[:], in0=xtld[:], in1=wlnbc[:, C:2 * C], op=ALU.mult))

                @block.tensor
                def _(te, tt=tt, pb=pb):
                    cx.wait(te, "ZRm", tt + 1)
                    if tt == 0:
                        cx.wait(te, "LD0", LD0v)
                    for i, src in enumerate([xhat, x1n[pb], xtld]):
                        for ct in range(CT):
                            gi = (tt * 3 + i) * CT + ct
                            if gi >= 2:
                                cx.wait(te, "TPD", gi - 1)
                            mm = te.transpose(out=ptpD[gi % 2][:, 0:128],
                                              in_=src[:, ct * 128:(ct + 1) * 128], identity=sid[:])
                            cx.inc("TPDm", mm)

                @block.vector
                def _(ve, tt=tt):
                    for i, dsts in enumerate([xhT, xnT1v, xtT]):
                        for ct in range(CT):
                            gi = (tt * 3 + i) * CT + ct
                            cx.wait(ve, "TPDm", gi + 1)
                            cx.inc("TPD", ve.tensor_copy(out=dsts[ct][:, tt * 128:(tt + 1) * 128],
                                                         in_=ptpD[gi % 2][:, 0:128]))

        @block.sync
        def _(sy):
            cx.wait(sy, "TPD", TT * 3 * CT)
            for ct in range(CT):
                cx.inc("SPL", sy.dma_start(out=xft_s[ct * 128:(ct + 1) * 128, :], in_=xtT[ct][:]), 16)
                cx.inc("SPL", sy.dma_start(out=xnt_s[ct * 128:(ct + 1) * 128, :], in_=xnT1v[ct][:]), 16)
            cx.wait(sy, "SPL", 16 * 2 * CT)
        bar()
        if _cut(1): return nc

        with nc.psum_tensor("pq0", [128, 512], F32) as pq0, \
             nc.psum_tensor("pq1", [128, 512], F32) as pq1, \
             nc.psum_tensor("pba", [128, 512], F32) as pba:
            pq = [pq0, pq1]
            for db in range(24):
                @block.gpsimd
                def _(gp, db=db):
                    if db >= 2:
                        cx.wait(gp, "QKm", 4 * (db - 1))
                    cx.inc("LQK", gp.dma_start(out=wblk[db % 2][:], in_=wqkvz[db]), 16)

                for tc in range(4):
                    gi2 = db * 4 + tc
                    @block.tensor
                    def _(te, db=db, tc=tc, gi2=gi2):
                        if tc == 0:
                            cx.wait(te, "LQK", 16 * (db + 1))
                        if gi2 >= 2:
                            cx.wait(te, "QKe", gi2 - 1)
                        for ct in range(CT):
                            mm = te.matmul(out=pq[gi2 % 2][:], lhsT=wblk[db % 2][:, ct * 128:(ct + 1) * 128],
                                           rhs=xhT[ct][:, tc * 512:(tc + 1) * 512],
                                           start=(ct == 0), stop=(ct == CT - 1))
                        cx.inc("QKm", mm)

                    @block.scalar
                    def _(sc, db=db, tc=tc, gi2=gi2):
                        cx.wait(sc, "QKm", gi2 + 1)
                        if db < 16:
                            if db >= 2 and tc == 0:
                                cx.wait(sc, "CVD", db - 1)
                            dst = craw[db % 2][:, 4 + tc * 512:4 + (tc + 1) * 512]
                        else:
                            dst = zT[db - 16][:, tc * 512:(tc + 1) * 512]
                        cx.inc("QKe", sc.copy(out=dst, in_=pq[gi2 % 2][:]))

                if db < 16:
                    @block.vector
                    def _(ve, db=db):
                        cx.wait(ve, "QKe", 4 * (db + 1))
                        if db == 0:
                            cx.wait(ve, "LD0", LD0v)
                        if db >= 2:
                            cx.wait(ve, "CVDs", db - 1)
                        w4 = lambda j: cvw[:, db * 4 + j:db * 4 + j + 1]
                        ve.memset(craw[db % 2][:, 1:4], 0.0)
                        ve.tensor_scalar(out=cvsc[0][:], in0=craw[db % 2][:, 1:1 + T], scalar1=w4(0), scalar2=None, op0=ALU.mult)
                        ve.scalar_tensor_tensor(out=cvsc[1][:], in0=craw[db % 2][:, 2:2 + T], scalar=w4(1), in1=cvsc[0][:], op0=ALU.mult, op1=ALU.add)
                        ve.scalar_tensor_tensor(out=cvsc[0][:], in0=craw[db % 2][:, 3:3 + T], scalar=w4(2), in1=cvsc[1][:], op0=ALU.mult, op1=ALU.add)
                        mm = ve.scalar_tensor_tensor(out=cvsc[1][:], in0=craw[db % 2][:, 4:4 + T], scalar=w4(3), in1=cvsc[0][:], op0=ALU.mult, op1=ALU.add)
                        cx.inc("CVD", mm)

                    @block.scalar
                    def _(sc, db=db):
                        cx.wait(sc, "CVD", db + 1)
                        cx.inc("CVDs", sc.activation(out=qkvc[db][:], in_=cvsc[1][:], func=AF.Silu))

            for tc in range(4):
                @block.tensor
                def _(te, tc=tc):
                    cx.wait(te, "LD0g", LD0gv)
                    cx.wait(te, "ISL", 1)
                    if tc > 0:
                        cx.wait(te, "BAe", 2 * tc)
                    for ct in range(CT):
                        mm = te.matmul(out=pba[0:8, :], lhsT=wbabS[:, ct * 8:(ct + 1) * 8],
                                       rhs=xhT[ct][:, tc * 512:(tc + 1) * 512],
                                       start=(ct == 0), stop=(ct == CT - 1))
                    cx.inc("BAm", mm)

                @block.scalar
                def _(sc, tc=tc):
                    cx.wait(sc, "BAm", 2 * tc + 1)
                    cx.inc("BAe", sc.copy(out=rows8[:, tc * 512:(tc + 1) * 512], in_=pba[0:8, :]))

                @block.tensor
                def _(te, tc=tc):
                    cx.wait(te, "BAe", 2 * tc + 1)
                    for ct in range(CT):
                        mm = te.matmul(out=pba[0:8, :], lhsT=wbaaS[:, ct * 8:(ct + 1) * 8],
                                       rhs=xhT[ct][:, tc * 512:(tc + 1) * 512],
                                       start=(ct == 0), stop=(ct == CT - 1))
                    cx.inc("BAm", mm)

                @block.scalar
                def _(sc, tc=tc):
                    cx.wait(sc, "BAm", 2 * tc + 2)
                    cx.inc("BAe", sc.copy(out=rowsA[:, tc * 512:(tc + 1) * 512], in_=pba[0:8, :]))
        bar()
        if _cut(2): return nc

        @block.scalar
        def _(sc):
            cx.wait(sc, "ALG", 1)
            sc.activation(out=beta8[:], in_=rows8[:], func=AF.Sigmoid)
            sc.activation(out=rows8[:], in_=rowsA[:], func=AF.Exp, bias=dtbs[:])
            cx.inc("GR1", sc.activation(out=rowsA[:], in_=rows8[:], func=AF.Ln, bias=1.0))

        @block.vector
        def _(ve):
            cx.wait(ve, "GR1", 1)
            cx.bump(ve, "v", ve.tensor_scalar(out=rowsA[:], in0=rowsA[:], scalar1=alg[:], scalar2=None, op0=ALU.mult))
            for ch in range(16):
                cx.bump(ve, "v", ve.tensor_tensor_scan(out=crel8[:, ch * 128:(ch + 1) * 128],
                                                       data0=ones8[:], data1=rowsA[:, ch * 128:(ch + 1) * 128],
                                                       initial=0.0, op0=ALU.mult, op1=ALU.add))
            for ch in range(16):
                cx.bump(ve, "v", ve.tensor_scalar(out=cend8[:, ch * 128:(ch + 1) * 128], in0=ones8[:],
                                                  scalar1=crel8[:, ch * 128 + 127:ch * 128 + 128],
                                                  scalar2=None, op0=ALU.mult))
            cx.inc("GR2", ve.engine_nop())

        with nc.psum_tensor("pnrm", [128, 512], F32) as pnrm, \
             nc.psum_tensor("pnbc", [128, 512], F32) as pnbc:
            for ht in range(8):
                for tc in range(4):
                    gi3 = ht * 4 + tc
                    @block.scalar
                    def _(sc, ht=ht, tc=tc, gi3=gi3):
                        if gi3 >= 1:
                            cx.wait(sc, "NRm", gi3)
                        cx.inc("NRs", sc.activation(out=sqn[:], in_=qkvc[ht][:, tc * 512:(tc + 1) * 512],
                                                    func=AF.Square))

                    @block.tensor
                    def _(te, gi3=gi3):
                        cx.wait(te, "NRs", gi3 + 1)
                        if gi3 >= 1:
                            cx.wait(te, "NRr", gi3)
                        mm = te.matmul(out=pnrm[0:1, :], lhsT=onesc[:], rhs=sqn[:], start=True, stop=True)
                        cx.inc("NRm", mm)

                    @block.vector
                    def _(ve, gi3=gi3):
                        cx.wait(ve, "NRm", gi3 + 1)
                        cx.bump(ve, "v", ve.tensor_scalar_add(out=nrow[0:1, :], in0=pnrm[0:1, :], scalar1=1e-6))
                        cx.inc("NRr", ve.engine_nop())

                    @block.scalar
                    def _(sc, gi3=gi3):
                        cx.wait(sc, "NRr", gi3 + 1)
                        cx.bump(sc, "s", sc.activation(out=nrow[0:1, :], in_=nrow[0:1, :], func=AF.Ln))
                        cx.inc("NRe", sc.activation(out=nrowb[0:1, :], in_=nrow[0:1, :], func=AF.Exp, scale=-0.5))

                    @block.tensor
                    def _(te, gi3=gi3):
                        cx.wait(te, "NRe", gi3 + 1)
                        if gi3 >= 1:
                            cx.wait(te, "NRdone", 2 * gi3)
                        mm = te.matmul(out=pnbc[:], lhsT=onesr[:], rhs=nrowb[0:1, :], start=True, stop=True)
                        cx.inc("NBm", mm)

                    @block.vector
                    def _(ve, ht=ht, tc=tc, gi3=gi3):
                        cx.wait(ve, "NBm", gi3 + 1)
                        ve.tensor_tensor(out=qkvc[ht][:, tc * 512:(tc + 1) * 512],
                                         in0=qkvc[ht][:, tc * 512:(tc + 1) * 512], in1=pnbc[:], op=ALU.mult)
                        if ht < 4:
                            mm = ve.tensor_scalar_mul(out=qkvc[ht][:, tc * 512:(tc + 1) * 512],
                                                      in0=qkvc[ht][:, tc * 512:(tc + 1) * 512],
                                                      scalar1=1.0 / (DK ** 0.5))
                            cx.inc("NRdone", mm, 2)
                        else:
                            cx.inc("NRdone", ve.engine_nop(), 2)

        for i in range(8):
            @block.scalar
            def _(sc, i=i):
                cx.inc("SZ", sc.activation(out=zT[i][:], in_=zT[i][:], func=AF.Silu))
        bar()
        if _cut(3): return nc

        with nc.psum_tensor("pcT", [128, 512], F32) as pcT:
            for ch in range(16):
                for gi4, (src, dstoff, bp) in enumerate([(beta8, 0, 0), (cend8, 8, 0), (crel8, 16, 0)]):
                    g7 = ch * 3 + gi4
                    @block.tensor
                    def _(te, ch=ch, src=src, g7=g7, bp=bp):
                        if g7 >= 1:
                            cx.wait(te, "BGe", g7)
                        mm = te.transpose(out=pcT[:, 0:8], in_=src[:, ch * 128:(ch + 1) * 128],
                                          identity=sidf[bp:bp + 8, bp:bp + 8])
                        cx.inc("BGm", mm)

                    @block.vector
                    def _(ve, ch=ch, dstoff=dstoff, g7=g7):
                        cx.wait(ve, "BGm", g7 + 1)
                        cx.inc("BGe", ve.tensor_copy(out=bgcol[:, ch * 24 + dstoff:ch * 24 + dstoff + 8],
                                                     in_=pcT[:, 0:8]))

        @block.scalar
        def _(sc):
            cx.wait(sc, "BGe", 48)
            cx.inc("BG3", sc.nop())

        with nc.psum_tensor("pKKf", [128, 512], F32) as pKKf, \
             nc.psum_tensor("pEf", [128, 512], F32) as pEf, \
             nc.psum_tensor("pMMf", [128, 512], F32) as pMMf, \
             nc.psum_tensor("pBBf", [128, 512], F32) as pBBf, \
             nc.psum_tensor("pTTf", [128, 1024], BF16) as pTTf, \
             nc.psum_tensor("pRWf", [128, 512], F32) as pRWf:
            pKK = pKKf[:, 0:128]
            pE = pEf[:, 0:128]
            pMM = pMMf[:, 0:128]
            pBB = pBBf[:, 0:128]
            pTT = pTTf[:, 0:128]

            def TE(f):
                block.tensor(f)

            def VE(f):
                block.vector(f)

            def SC(f):
                block.scalar(f)

            for j in range(HV):
                qh = j // 2
                for ch in range(16):
                    c0 = ch * 128
                    first = (j == 0 and ch == 0)
                    Kc = qkvc[4 + qh][:, c0:c0 + 128]
                    Qc = qkvc[qh][:, c0:c0 + 128]
                    Vc = qkvc[8 + j][:, c0:c0 + 128]
                    Scur = dS[2 * j + (ch % 2)]
                    Snext = dS[2 * j + ((ch + 1) % 2)]
                    bpcol = bgcol[:, ch * 24 + j:ch * 24 + j + 1]
                    crcol = bgcol[:, ch * 24 + 16 + j:ch * 24 + 16 + j + 1]
                    cecol = bgcol[:, ch * 24 + 8 + j:ch * 24 + 8 + j + 1]
                    gcol = ccol[:, 4:5]
                    gecol = ccol[:, 5:6]

                    def vrows(ve, bpcol=bpcol, crcol=crcol, cecol=cecol, first=first):
                        if first:
                            cx.wait(ve, "BG3", 1)
                            cx.wait(ve, "NRdone", 64)
                            cx.wait(ve, "SZ", 8)
                        else:
                            cx.wait(ve, "dm", cx.val("dm"))
                        cx.bump(ve, "v", ve.tensor_scalar_mul(out=ccol[:, 1:2], in0=crcol, scalar1=-1.0))
                        cx.bump(ve, "v", ve.tensor_tensor(out=ccol[:, 3:4], in0=cecol, in1=ccol[:, 1:2], op=ALU.add))
                        cx.inc("dv", ve.engine_nop())
                    VE(vrows)

                    def srows(sc, bpcol=bpcol, crcol=crcol, cecol=cecol):
                        cx.wait(sc, "dv", cx.val("dv"))
                        cx.bump(sc, "s", sc.activation(out=ccol[:, 3:4], in_=ccol[:, 3:4], func=AF.Exp))
                        cx.bump(sc, "s", sc.activation(out=ccol[:, 4:5], in_=crcol, func=AF.Exp))
                        cx.bump(sc, "s", sc.activation(out=ccol[:, 5:6], in_=cecol, func=AF.Exp))
                        cx.inc("ds", sc.activation(out=ccol[:, 0:1], in_=ccol[:, 4:5], func=AF.Identity, scale=bpcol))
                    SC(srows)

                    def trows(te, bpcol=bpcol):
                        cx.wait(te, "ds", cx.val("ds"))
                        te.transpose(out=pRWf[0:1, 0:128], in_=ccol[:, 0:1], identity=sidf[:])
                        te.transpose(out=pRWf[0:1, 128:256], in_=ccol[:, 1:2], identity=sidf[:])
                        te.transpose(out=pRWf[0:1, 256:384], in_=ccol[:, 3:4], identity=sidf[:])
                        m = te.transpose(out=pRWf[0:1, 384:512], in_=bpcol, identity=sidf[:])
                        cx.inc("dm", m)
                    TE(trows)

                    def vrows2(ve):
                        cx.wait(ve, "dm", cx.val("dm"))
                        ve.tensor_copy(out=rowp[0:1, 0:128], in_=pRWf[0:1, 0:128])
                        ve.tensor_copy(out=rowpf[0:1, 0:128], in_=pRWf[0:1, 128:256])
                        ve.tensor_copy(out=rowp[0:1, 128:256], in_=pRWf[0:1, 256:384])
                        cx.inc("dv", ve.tensor_copy(out=rowp[0:1, 256:384], in_=pRWf[0:1, 384:512]))
                    VE(vrows2)

                    def te1(te, Kc=Kc, Qc=Qc):
                        cx.wait(te, "dv", cx.val("dv"))
                        m = te.matmul(out=pKK, lhsT=Kc, rhs=Kc, start=True, stop=True)
                        cx.inc("dm", m)
                        m = te.matmul(out=pE, lhsT=onesf[:], rhs=rowpf[0:1, 0:128], start=True, stop=True)
                        cx.inc("dm", m)
                        m = te.matmul(out=pMM, lhsT=Qc, rhs=Kc, start=True, stop=True)
                        cx.inc("dm", m)
                    TE(te1)

                    def ve1(ve, crcol=crcol):
                        cx.wait(ve, "dm", cx.val("dm"))
                        cx.inc("dv", ve.tensor_scalar(out=semi[:], in0=pE, scalar1=crcol, scalar2=0.0,
                                                      op0=ALU.add, op1=ALU.min))
                    VE(ve1)

                    def sc1(sc):
                        cx.wait(sc, "dv", cx.val("dv"))
                        cx.inc("ds", sc.activation(out=dxp[:], in_=semi[:], func=AF.Exp))
                    SC(sc1)

                    def ve2(ve, bpcol=bpcol):
                        cx.wait(ve, "ds", cx.val("ds"))
                        ve.tensor_tensor(out=dg[:], in0=dxp[:], in1=pKK, op=ALU.mult)
                        ve.tensor_tensor(out=daq[:], in0=dxp[:], in1=pMM, op=ALU.mult)
                        ve.tensor_tensor(out=daq[:], in0=daq[:], in1=smask[:, 128:256], op=ALU.mult)
                        ve.tensor_tensor(out=dn[:], in0=dg[:], in1=smask[:, 0:128], op=ALU.mult)
                        ve.tensor_scalar(out=dn[:], in0=dn[:], scalar1=bpcol, scalar2=-1.0, op0=ALU.mult, op1=ALU.mult)
                        ve.tensor_tensor(out=dprod[:], in0=dn[:], in1=sid[:], op=ALU.add)
                        cx.inc("dv", ve.tensor_copy(out=dcur[:], in_=dn[:]))
                    VE(ve2)

                    for step in range(6):
                        def teS(te):
                            cx.wait(te, "dv", cx.val("dv"))
                            cx.inc("dm", te.transpose(out=pTT, in_=dcur[:], identity=sid[:]))
                        TE(teS)

                        def veS(ve):
                            cx.wait(ve, "dm", cx.val("dm"))
                            cx.inc("dv", ve.tensor_copy(out=dtmp[:], in_=pTT))
                        VE(veS)

                        def teS2(te):
                            cx.wait(te, "dv", cx.val("dv"))
                            cx.inc("dm", te.matmul(out=pMM, lhsT=dtmp[:], rhs=dcur[:], start=True, stop=True))
                        TE(teS2)

                        def veS2(ve):
                            cx.wait(ve, "dm", cx.val("dm"))
                            cx.inc("dv", ve.tensor_copy(out=dcur[:], in_=pMM))
                        VE(veS2)

                        def teS3(te):
                            cx.wait(te, "dv", cx.val("dv"))
                            cx.inc("dm", te.transpose(out=pTT, in_=dprod[:], identity=sid[:]))
                        TE(teS3)

                        def veS3(ve):
                            cx.wait(ve, "dm", cx.val("dm"))
                            cx.inc("dv", ve.tensor_copy(out=dtmp2[:], in_=pTT))
                        VE(veS3)

                        def teS4(te):
                            cx.wait(te, "dv", cx.val("dv"))
                            cx.inc("dm", te.matmul(out=pMM, lhsT=dtmp2[:], rhs=dcur[:], start=True, stop=True))
                        TE(teS4)

                        def veS4(ve):
                            cx.wait(ve, "dm", cx.val("dm"))
                            cx.inc("dv", ve.tensor_tensor(out=dprod[:], in0=dprod[:], in1=pMM, op=ALU.add))
                        VE(veS4)

                    def te1b(te):
                        cx.wait(te, "dv", cx.val("dv"))
                        cx.inc("dm", te.matmul(out=pBB, lhsT=onesr[:], rhs=rowp[0:1, 0:128], start=True, stop=True))
                    TE(te1b)

                    def ve1b(ve, Kc=Kc):
                        cx.wait(ve, "dm", cx.val("dm"))
                        cx.inc("dv", ve.tensor_tensor(out=dkg[:], in0=Kc, in1=pBB, op=ALU.mult))
                    VE(ve1b)

                    def te1c(te, ch=ch, Scur=Scur):
                        cx.wait(te, "dv", cx.val("dv"))
                        if ch > 0:
                            cx.inc("dm", te.matmul(out=pMM, lhsT=Scur[:], rhs=dkg[:], start=True, stop=True))
                        cx.inc("dm", te.matmul(out=pBB, lhsT=onesr[:], rhs=rowp[0:1, 256:384], start=True, stop=True))
                    TE(te1c)

                    def ve1c(ve, Vc=Vc, ch=ch):
                        cx.wait(ve, "dm", cx.val("dm"))
                        ve.tensor_tensor(out=drt[:], in0=Vc, in1=pBB, op=ALU.mult)
                        if ch > 0:
                            m = ve.tensor_tensor(out=drt[:], in0=drt[:], in1=pMM, op=ALU.subtract)
                        else:
                            m = ve.tensor_copy(out=dtmp[:], in_=drt[:])
                        cx.inc("dv", m)
                    VE(ve1c)

                    def te2(te):
                        cx.wait(te, "dv", cx.val("dv"))
                        cx.inc("dm", te.transpose(out=pTT, in_=drt[:], identity=sid[:]))
                    TE(te2)

                    def ve2b(ve):
                        cx.wait(ve, "dm", cx.val("dm"))
                        cx.inc("dv", ve.tensor_copy(out=dtmp[:], in_=pTT))
                    VE(ve2b)

                    def te2b(te):
                        cx.wait(te, "dv", cx.val("dv"))
                        cx.inc("dm", te.transpose(out=pTT, in_=dprod[:], identity=sid[:]))
                    TE(te2b)

                    def ve2c(ve):
                        cx.wait(ve, "dm", cx.val("dm"))
                        cx.inc("dv", ve.tensor_copy(out=dtmp2[:], in_=pTT))
                    VE(ve2c)

                    def te2c(te):
                        cx.wait(te, "dv", cx.val("dv"))
                        cx.inc("dm", te.matmul(out=pMM, lhsT=dtmp2[:], rhs=dtmp[:], start=True, stop=True))
                    TE(te2c)

                    def ve2d(ve):
                        cx.wait(ve, "dm", cx.val("dm"))
                        cx.inc("dv", ve.tensor_copy(out=dde[:], in_=pMM))
                    VE(ve2d)

                    def te3(te):
                        cx.wait(te, "dv", cx.val("dv"))
                        cx.inc("dm", te.transpose(out=pTT, in_=daq[:], identity=sid[:]))
                    TE(te3)

                    def ve3(ve):
                        cx.wait(ve, "dm", cx.val("dm"))
                        cx.inc("dv", ve.tensor_copy(out=dtmp2[:], in_=pTT))
                    VE(ve3)

                    def te3b(te, ch=ch, Qc=Qc, Scur=Scur):
                        cx.wait(te, "dv", cx.val("dv"))
                        cx.inc("dm", te.matmul(out=pMM, lhsT=dtmp2[:], rhs=dde[:], start=True, stop=True))
                        if ch > 0:
                            cx.inc("dm", te.matmul(out=pBB, lhsT=Qc, rhs=Scur[:], start=True, stop=True))
                    TE(te3b)

                    def ve3b(ve, ch=ch, gcol=gcol):
                        cx.wait(ve, "dm", cx.val("dm"))
                        if ch > 0:
                            ve.tensor_scalar(out=don[:], in0=pBB, scalar1=gcol, scalar2=None, op0=ALU.mult)
                            m = ve.tensor_tensor(out=don[:], in0=don[:], in1=pMM, op=ALU.add)
                        else:
                            m = ve.tensor_copy(out=don[:], in_=pMM)
                        cx.inc("dv", m)
                    VE(ve3b)

                    def sc2(sc):
                        cx.wait(sc, "dv", cx.val("dv"))
                        cx.inc("ds", sc.activation(out=dtmp[:], in_=don[:], func=AF.Square))
                    SC(sc2)

                    def ve4(ve):
                        cx.wait(ve, "ds", cx.val("ds"))
                        cx.bump(ve, "v", ve.reduce_sum(out=dcol[:, 0:1], in_=dtmp[:], axis=AX.X))
                        cx.bump(ve, "v", ve.tensor_scalar(out=dcol[:, 0:1], in0=dcol[:, 0:1],
                                                          scalar1=1.0 / DV, scalar2=1e-6, op0=ALU.mult, op1=ALU.add))
                        cx.inc("dv", ve.engine_nop())
                    VE(ve4)

                    def sc3(sc):
                        cx.wait(sc, "dv", cx.val("dv"))
                        cx.bump(sc, "s", sc.activation(out=dcol[:, 1:2], in_=dcol[:, 0:1], func=AF.Ln))
                        cx.bump(sc, "s", sc.activation(out=dcol[:, 2:3], in_=dcol[:, 1:2], func=AF.Exp, scale=-0.5))
                        cx.inc("ds", sc.nop())
                    SC(sc3)

                    def ve5(ve):
                        cx.wait(ve, "ds", cx.val("ds"))
                        ve.tensor_scalar(out=don[:], in0=don[:], scalar1=dcol[:, 2:3], scalar2=None, op0=ALU.mult)
                        cx.inc("dv", ve.tensor_tensor(out=don[:], in0=don[:], in1=gnwb[:], op=ALU.mult))
                    VE(ve5)

                    def te4(te):
                        cx.wait(te, "dv", cx.val("dv"))
                        cx.inc("dm", te.transpose(out=pTT, in_=don[:], identity=sid[:]))
                    TE(te4)

                    def ve6(ve, j=j, c0=c0):
                        cx.wait(ve, "dm", cx.val("dm"))
                        cx.inc("dv", ve.tensor_tensor(out=coreT[j][:, c0:c0 + 128], in0=zT[j][:, c0:c0 + 128],
                                                      in1=pTT, op=ALU.mult))
                    VE(ve6)

                    if ch < 15:
                        def te5(te):
                            cx.wait(te, "dv", cx.val("dv"))
                            cx.inc("dm", te.matmul(out=pBB, lhsT=onesr[:], rhs=rowp[0:1, 128:256], start=True, stop=True))
                        TE(te5)

                        def ve7(ve, Kc=Kc):
                            cx.wait(ve, "dm", cx.val("dm"))
                            cx.inc("dv", ve.tensor_tensor(out=dkg[:], in0=Kc, in1=pBB, op=ALU.mult))
                        VE(ve7)

                        def te5b(te):
                            cx.wait(te, "dv", cx.val("dv"))
                            cx.inc("dm", te.transpose(out=pTT, in_=dkg[:], identity=sid[:]))
                        TE(te5b)

                        def ve7b(ve):
                            cx.wait(ve, "dm", cx.val("dm"))
                            cx.inc("dv", ve.tensor_copy(out=dtmp[:], in_=pTT))
                        VE(ve7b)

                        def te5c(te):
                            cx.wait(te, "dv", cx.val("dv"))
                            cx.inc("dm", te.matmul(out=pMM, lhsT=dtmp[:], rhs=dde[:], start=True, stop=True))
                        TE(te5c)

                        def ve7c(ve, ch=ch, Scur=Scur, Snext=Snext, gecol=gecol):
                            cx.wait(ve, "dm", cx.val("dm"))
                            if ch > 0:
                                m = ve.scalar_tensor_tensor(out=Snext[:], in0=Scur[:], scalar=gecol,
                                                            in1=pMM, op0=ALU.mult, op1=ALU.add)
                            else:
                                m = ve.tensor_copy(out=Snext[:], in_=pMM)
                            cx.inc("dv", m)
                        VE(ve7c)
        bar()
        if _cut(4): return nc

        @block.sync
        def _(sy):
            for ct in range(CT):
                cx.inc("LXN", sy.dma_start(out=hT[ct][:], in_=xnt_s[ct * 128:(ct + 1) * 128, :]), 16)
        with nc.psum_tensor("pw4a", [128, 512], F32) as pw4a, \
             nc.psum_tensor("pw4b", [128, 512], F32) as pw4b:
            pw4 = [pw4a, pw4b]
            for co in range(CT):
                @block.gpsimd
                def _(gp, co=co):
                    if co >= 2:
                        cx.wait(gp, "W4m", 4 * (co - 1))
                    cx.inc("LW4", gp.dma_start(out=wblk[co % 2][:], in_=woutD[co]), 16)
                for tc in range(4):
                    g4 = co * 4 + tc
                    @block.tensor
                    def _(te, co=co, tc=tc, g4=g4):
                        if tc == 0:
                            cx.wait(te, "LW4", 16 * (co + 1))
                        if g4 >= 2:
                            cx.wait(te, "W4e", 2 * (g4 - 1))
                        for vd in range(8):
                            mm = te.matmul(out=pw4[g4 % 2][:], lhsT=wblk[co % 2][:, vd * 128:(vd + 1) * 128],
                                           rhs=coreT[vd][:, tc * 512:(tc + 1) * 512],
                                           start=(vd == 0), stop=(vd == 7))
                        cx.inc("W4m", mm)

                    @block.vector
                    def _(ve, co=co, tc=tc, g4=g4):
                        cx.wait(ve, "W4m", g4 + 1)
                        cx.wait(ve, "LXN", 16 * CT)
                        ve.tensor_copy(out=outT[co][tc // 2][:, (tc % 2) * 512:(tc % 2) * 512 + 512],
                                       in_=pw4[g4 % 2][:])
                        mm = ve.tensor_tensor(out=hT[co][:, tc * 512:(tc + 1) * 512],
                                              in0=hT[co][:, tc * 512:(tc + 1) * 512],
                                              in1=pw4[g4 % 2][:], op=ALU.add)
                        cx.inc("W4e", mm, 2)
        bar()
        if _cut(5): return nc
        _dump([(hT_dbg[i], hT[i][:]) for i in range(8)] if debug else [])

        def tnorm(tag, srcT, outT_t, wimg_ap, eps):
            wsb = sbt("w_" + tag, [128, CT], F32)

            @block.sync
            def _(sy, tag=tag):
                cx.inc(tag + "L", sy.dma_start(out=wsb[:], in_=wimg_ap), 16)

            with nc.psum_tensor("pn_" + tag, [128, 512], F32) as pnr, \
                 nc.psum_tensor("pn2_" + tag, [128, 512], F32) as pnb:
                for tc in range(4):
                    for ct in range(CT):
                        g5 = tc * CT + ct
                        @block.scalar
                        def _(sc, tc=tc, ct=ct, g5=g5, tag=tag):
                            if g5 >= 1:
                                cx.wait(sc, tag + "m", g5)
                            cx.inc(tag + "s", sc.activation(out=sqn[:], in_=srcT[ct][:, tc * 512:(tc + 1) * 512],
                                                            func=AF.Square))

                        @block.tensor
                        def _(te, tc=tc, ct=ct, g5=g5, tag=tag):
                            cx.wait(te, tag + "s", g5 + 1)
                            if ct == 0 and tc > 0:
                                cx.wait(te, tag + "r", tc)
                            cx.inc(tag + "m", te.matmul(out=pnr[0:1, :], lhsT=onesc[:], rhs=sqn[:],
                                                        start=(ct == 0), stop=(ct == CT - 1)))

                    @block.vector
                    def _(ve, tc=tc, tag=tag):
                        cx.wait(ve, tag + "m", (tc + 1) * CT)
                        cx.bump(ve, "v", ve.tensor_scalar(out=nrow[0:1, :], in0=pnr[0:1, :],
                                                          scalar1=1.0 / C, scalar2=eps, op0=ALU.mult, op1=ALU.add))
                        cx.inc(tag + "r", ve.engine_nop())

                    @block.scalar
                    def _(sc, tc=tc, tag=tag):
                        cx.wait(sc, tag + "r", tc + 1)
                        cx.bump(sc, "s", sc.activation(out=nrow[0:1, :], in_=nrow[0:1, :], func=AF.Ln))
                        cx.bump(sc, "s", sc.activation(out=nrowb[0:1, :], in_=nrow[0:1, :], func=AF.Exp, scale=-0.5))
                        cx.inc(tag + "e", sc.nop())

                    @block.tensor
                    def _(te, tc=tc, tag=tag):
                        cx.wait(te, tag + "e", tc + 1)
                        cx.wait(te, tag + "L", 16)
                        if tc >= 1:
                            cx.wait(te, tag + "w", tc * CT)
                        cx.inc(tag + "b", te.matmul(out=pnb[:], lhsT=onesr[:], rhs=nrowb[0:1, :],
                                                    start=True, stop=True))

                    @block.vector
                    def _(ve, tc=tc, tag=tag):
                        cx.wait(ve, tag + "b", tc + 1)
                        for ct in range(CT):
                            mm = ve.scalar_tensor_tensor(out=outT_t[ct][:, tc * 512:(tc + 1) * 512],
                                                         in0=srcT[ct][:, tc * 512:(tc + 1) * 512],
                                                         scalar=wsb[:, ct:ct + 1],
                                                         in1=pnb[:], op0=ALU.mult, op1=ALU.mult)
                            cx.inc(tag + "w", mm)

        def moe(tag, xT, wrt_sb, use_bias, wg_ap, wu_ap, wd_ap, shared):
            with nc.psum_tensor("pr_" + tag, [128, 512], F32) as prt_, \
                 nc.psum_tensor("pt_" + tag, [128, 512], F32) as ptt_:
                for tc in range(4):
                    @block.tensor
                    def _(te, tc=tc, tag=tag):
                        if tc > 0:
                            cx.wait(te, tag + "Re", tc)
                        for ct in range(CT):
                            mm = te.matmul(out=prt_[0:8, :], lhsT=wrt_sb[:, ct * 8:(ct + 1) * 8],
                                           rhs=xT[ct][:, tc * 512:(tc + 1) * 512],
                                           start=(ct == 0), stop=(ct == CT - 1))
                        cx.inc(tag + "Rm", mm)

                    @block.scalar
                    def _(sc, tc=tc, tag=tag):
                        cx.wait(sc, tag + "Rm", tc + 1)
                        if use_bias:
                            cx.inc(tag + "Re", sc.activation(out=lgT[:, tc * 512:(tc + 1) * 512],
                                                             in_=prt_[0:8, :], func=AF.Identity, bias=ebiasS[:]))
                        else:
                            cx.inc(tag + "Re", sc.copy(out=lgT[:, tc * 512:(tc + 1) * 512], in_=prt_[0:8, :]))

                for tt2 in range(TT):
                    @block.tensor
                    def _(te, tt2=tt2, tag=tag):
                        cx.wait(te, tag + "Re", 4)
                        if tt2 >= 1:
                            cx.wait(te, tag + "Te", tt2)
                        cx.inc(tag + "Tm", te.transpose(out=ptt_[:, 0:8], in_=lgT[0:8, tt2 * 128:(tt2 + 1) * 128],
                                                        identity=sidf[0:8, 0:8]))

                    @block.vector
                    def _(ve, tt2=tt2, tag=tag):
                        cx.wait(ve, tag + "Tm", tt2 + 1)
                        cx.bump(ve, "v", ve.tensor_copy(out=lg8[:], in_=ptt_[:, 0:8]))
                        cx.inc(tag + "Te", ve.engine_nop())
                        cx.bump(ve, "v", ve.reduce_max(out=dcol[:, 0:1], in_=lg8[:], axis=AX.X))
                        cx.bump(ve, "v", ve.tensor_scalar(out=oh8[:], in0=lg8[:], scalar1=dcol[:, 0:1],
                                                          scalar2=None, op0=ALU.is_equal))
                        cx.bump(ve, "v", ve.tensor_scalar_mul(out=oh8b[:], in0=oh8[:], scalar1=BIG))
                        cx.bump(ve, "v", ve.tensor_tensor(out=lg8[:], in0=lg8[:], in1=oh8b[:], op=ALU.subtract))
                        cx.bump(ve, "v", ve.reduce_max(out=dcol[:, 1:2], in_=lg8[:], axis=AX.X))
                        cx.bump(ve, "v", ve.tensor_scalar(out=oh8b[:], in0=lg8[:], scalar1=dcol[:, 1:2],
                                                          scalar2=None, op0=ALU.is_equal))
                        cx.bump(ve, "v", ve.tensor_tensor(out=dcol[:, 2:3], in0=dcol[:, 1:2], in1=dcol[:, 0:1],
                                                          op=ALU.subtract))
                        cx.inc(tag + "K1", ve.engine_nop())

                    @block.scalar
                    def _(sc, tt2=tt2, tag=tag):
                        cx.wait(sc, tag + "K1", tt2 + 1)
                        cx.inc(tag + "K2", sc.activation(out=dcol[:, 3:4], in_=dcol[:, 2:3], func=AF.Sigmoid,
                                                         scale=-1.0))

                    @block.vector
                    def _(ve, tt2=tt2, tag=tag):
                        cx.wait(ve, tag + "K2", tt2 + 1)
                        cx.bump(ve, "v", ve.tensor_scalar(out=dcol[:, 4:5], in0=dcol[:, 3:4], scalar1=-1.0,
                                                          scalar2=1.0, op0=ALU.mult, op1=ALU.add))
                        cx.bump(ve, "v", ve.tensor_scalar(out=oh8[:], in0=oh8[:], scalar1=dcol[:, 3:4],
                                                          scalar2=None, op0=ALU.mult))
                        cx.bump(ve, "v", ve.tensor_scalar(out=oh8b[:], in0=oh8b[:], scalar1=dcol[:, 4:5],
                                                          scalar2=None, op0=ALU.mult))
                        cx.bump(ve, "v", ve.tensor_tensor(out=oh8[:], in0=oh8[:], in1=oh8b[:], op=ALU.add))
                        cx.inc(tag + "K3", ve.tensor_copy(out=routeTM[:, tt2 * 8:(tt2 + 1) * 8], in_=oh8[:]))

            nex = NEX + (1 if shared else 0)
            with nc.psum_tensor("pg_" + tag, [128, 512], F32) as pgD, \
                 nc.psum_tensor("pu_" + tag, [128, 512], F32) as puD, \
                 nc.psum_tensor("pd_" + tag, [128, 512], F32) as pdD, \
                 nc.psum_tensor("pb_" + tag, [128, 512], F32) as pbD:
                for e in range(nex):
                    is_sh = (e == NEX)
                    if not is_sh:
                        for tt2 in range(TT):
                            @block.tensor
                            def _(te, e=e, tt2=tt2, tag=tag):
                                cx.wait(te, tag + "K3", TT)
                                if e > 0 and tt2 == 0:
                                    cx.wait(te, tag + "RBe", e * TT)
                                    cx.wait(te, tag + "RCe", e * 4)
                                if tt2 >= 1:
                                    cx.wait(te, tag + "RBe", e * TT + tt2)
                                cx.inc(tag + "RBm", te.transpose(out=pbD[0:1, 0:128],
                                                                 in_=routeTM[:, tt2 * 8 + e:tt2 * 8 + e + 1],
                                                                 identity=sidf[:]))

                            @block.vector
                            def _(ve, e=e, tt2=tt2, tag=tag):
                                cx.wait(ve, tag + "RBm", e * TT + tt2 + 1)
                                cx.inc(tag + "RBe", ve.tensor_copy(out=routeP[0:1, tt2 * 128:(tt2 + 1) * 128],
                                                                   in_=pbD[0:1, 0:128]))

                        for tc in range(4):
                            @block.tensor
                            def _(te, e=e, tc=tc, tag=tag):
                                cx.wait(te, tag + "RBe", (e + 1) * TT)
                                if tc >= 1:
                                    cx.wait(te, tag + "RCe", e * 4 + tc)
                                cx.inc(tag + "RCm", te.matmul(out=pbD[:], lhsT=onesr[:],
                                                              rhs=routeP[0:1, tc * 512:(tc + 1) * 512],
                                                              start=True, stop=True))

                            @block.vector
                            def _(ve, e=e, tc=tc, tag=tag):
                                cx.wait(ve, tag + "RCm", e * 4 + tc + 1)
                                cx.inc(tag + "RCe", ve.tensor_copy(out=rbcS[:, tc * 512:(tc + 1) * 512],
                                                                   in_=pbD[:]))

                    for ft in range(4):
                        fe = e * 4 + ft
                        @block.gpsimd
                        def _(gp, e=e, ft=ft, fe=fe, tag=tag, is_sh=is_sh):
                            if ft == 0 and e > 0:
                                cx.wait(gp, tag + "Dm", 32 * e)
                            if fe >= 2:
                                cx.wait(gp, tag + "Gm", 8 * (fe - 1))
                            src_g = wsg[ft] if is_sh else wg_ap[e, ft]
                            src_u = wsu[ft] if is_sh else wu_ap[e, ft]
                            cx.inc(tag + "Lg", gp.dma_start(out=wblk[ft % 2][:], in_=src_g), 16)
                            cx.inc(tag + "Lg", gp.dma_start(out=wblk2[ft % 2][:], in_=src_u), 16)

                        for tc in range(4):
                            g6 = fe * 4 + tc
                            @block.tensor
                            def _(te, ft=ft, tc=tc, fe=fe, g6=g6, tag=tag):
                                if tc == 0:
                                    cx.wait(te, tag + "Lg", 32 * (fe + 1))
                                if g6 >= 1:
                                    cx.wait(te, tag + "He", g6)
                                for ct in range(CT):
                                    te.matmul(out=pgD[:], lhsT=wblk[ft % 2][:, ct * 128:(ct + 1) * 128],
                                              rhs=xT[ct][:, tc * 512:(tc + 1) * 512],
                                              start=(ct == 0), stop=(ct == CT - 1))
                                for ct in range(CT):
                                    mm = te.matmul(out=puD[:], lhsT=wblk2[ft % 2][:, ct * 128:(ct + 1) * 128],
                                                   rhs=xT[ct][:, tc * 512:(tc + 1) * 512],
                                                   start=(ct == 0), stop=(ct == CT - 1))
                                cx.inc(tag + "Gm", mm, 2)

                            @block.scalar
                            def _(sc, g6=g6, tag=tag):
                                cx.wait(sc, tag + "Gm", 2 * (g6 + 1))
                                cx.inc(tag + "Hs", sc.activation(out=sqn[:], in_=pgD[:], func=AF.Silu))

                            @block.vector
                            def _(ve, e=e, ft=ft, tc=tc, g6=g6, tag=tag, is_sh=is_sh):
                                cx.wait(ve, tag + "Hs", g6 + 1)
                                ve.tensor_tensor(out=hE[ft][:, tc * 512:(tc + 1) * 512],
                                                 in0=sqn[:], in1=puD[:], op=ALU.mult)
                                if not is_sh:
                                    mm = ve.tensor_tensor(out=hE[ft][:, tc * 512:(tc + 1) * 512],
                                                          in0=hE[ft][:, tc * 512:(tc + 1) * 512],
                                                          in1=rbcS[:, tc * 512:(tc + 1) * 512], op=ALU.mult)
                                else:
                                    mm = ve.engine_nop()
                                cx.inc(tag + "He", mm)

                    @block.gpsimd
                    def _(gp, e=e, tag=tag, is_sh=is_sh):
                        if e >= 1:
                            cx.wait(gp, tag + "Dm", 32 * e)
                        cx.wait(gp, tag + "Gm", 8 * (e * 4 + 4))
                        for ft in range(4):
                            src = wsd[ft * 128:(ft + 1) * 128, :] if is_sh else wd_ap[e, ft * 128:(ft + 1) * 128, :]
                            cx.inc(tag + "Ld", gp.dma_start(out=wdq[ft][:], in_=src), 16)

                    for tc in range(4):
                        for co in range(CT):
                            g8 = (e * 4 + tc) * CT + co
                            @block.tensor
                            def _(te, e=e, tc=tc, co=co, g8=g8, tag=tag):
                                if tc == 0 and co == 0:
                                    cx.wait(te, tag + "Ld", 64 * (e + 1))
                                    cx.wait(te, tag + "He", (e * 4 + 4) * 4)
                                if g8 >= 1:
                                    cx.wait(te, tag + "De", g8)
                                for ft in range(4):
                                    mm = te.matmul(out=pdD[:], lhsT=wdq[ft][:, co * 128:(co + 1) * 128],
                                                   rhs=hE[ft][:, tc * 512:(tc + 1) * 512],
                                                   start=(ft == 0), stop=(ft == 3))
                                cx.inc(tag + "Dm", mm)

                            @block.vector
                            def _(ve, e=e, tc=tc, co=co, g8=g8, tag=tag):
                                cx.wait(ve, tag + "Dm", g8 + 1)
                                cx.inc(tag + "De", ve.tensor_tensor(
                                    out=outT[co][tc // 2][:, (tc % 2) * 512:(tc % 2) * 512 + 512],
                                    in0=outT[co][tc // 2][:, (tc % 2) * 512:(tc % 2) * 512 + 512],
                                    in1=pdD[:], op=ALU.add))

        tnorm("n2a", hT, hT, w_ln2a, 1e-6)
        bar()
        _dump([(x2_dbg[i], hT[i][:]) for i in range(8)] if debug else [])
        moe("m1", hT, wrtS, False, wge, wue, wde, shared=False)
        bar()
        if _cut(6): return nc
        _dump(([(o5_dbg[i], qkvc[i][:]) for i in range(8)]
               + [(rt_dbg, routeTM[:]), (lg_dbg, lgT[:])]
               + [(he_dbg[i], qkvc[8 + i][:]) for i in range(4)]) if debug else [])

        @block.sync
        def _(sy):
            for ct in range(CT):
                cx.inc("LXT", sy.dma_start(out=xaT[ct][:], in_=xft_s[ct * 128:(ct + 1) * 128, :]), 16)
            cx.inc("LCS", sy.dma_start(out=cosS[:], in_=cosb), 16)
            cx.inc("LCS", sy.dma_start(out=sinS[:], in_=sinb), 16)
            cx.inc("LCS", sy.dma_start(out=cm4[:], in_=cmask4), 16)
        with nc.psum_tensor("pa0", [128, 512], F32) as pa0, \
             nc.psum_tensor("pa1", [128, 512], F32) as pa1, \
             nc.psum_tensor("paB", [128, 1024], BF16) as paB:
            pa = [pa0, pa1]
            for db in range(8):
                @block.gpsimd
                def _(gp, db=db):
                    if db >= 2:
                        cx.wait(gp, "QFm", 4 * (db - 1))
                    cx.inc("LQF", gp.dma_start(out=wblk[db % 2][:], in_=wqD[db]), 16)
                for tc in range(4):
                    g9 = db * 4 + tc
                    @block.tensor
                    def _(te, db=db, tc=tc, g9=g9):
                        if tc == 0:
                            cx.wait(te, "LQF", 16 * (db + 1))
                        cx.wait(te, "LXT", 16 * CT)
                        if g9 >= 2:
                            cx.wait(te, "QFe", g9 - 1)
                        for ct in range(CT):
                            mm = te.matmul(out=pa[g9 % 2][:], lhsT=wblk[db % 2][:, ct * 128:(ct + 1) * 128],
                                           rhs=xaT[ct][:, tc * 512:(tc + 1) * 512],
                                           start=(ct == 0), stop=(ct == CT - 1))
                        cx.inc("QFm", mm)

                    @block.scalar
                    def _(sc, db=db, tc=tc, g9=g9):
                        cx.wait(sc, "QFm", g9 + 1)
                        cx.inc("QFe", sc.copy(out=qTf[db][:, tc * 512:(tc + 1) * 512], in_=pa[g9 % 2][:]))

            @block.gpsimd
            def _(gp):
                cx.wait(gp, "QFm", 32)
                cx.inc("LQF2", gp.dma_start(out=wblk[0][:], in_=wlat[0]), 16)
                cx.inc("LQF2", gp.dma_start(out=wblk2[0][:, 0:128], in_=wkre), 16)
                cx.inc("LQF2", gp.dma_start(out=wblk2[0][:, 128:256], in_=wkro), 16)
                cx.inc("LKV", gp.dma_start(out=wkvbS[:], in_=wkvb), 16)

            for tc in range(4):
                @block.tensor
                def _(te, tc=tc):
                    cx.wait(te, "LQF2", 48)
                    if tc == 0:
                        cx.wait(te, "QFe", 32)
                    if tc >= 2:
                        cx.wait(te, "LTe", tc - 1)
                    for ct in range(CT):
                        mm = te.matmul(out=pa[tc % 2][:], lhsT=wblk[0][:, ct * 128:(ct + 1) * 128],
                                       rhs=xaT[ct][:, tc * 512:(tc + 1) * 512],
                                       start=(ct == 0), stop=(ct == CT - 1))
                    cx.inc("LTm", mm)

                @block.scalar
                def _(sc, tc=tc):
                    cx.wait(sc, "LTm", tc + 1)
                    cx.inc("LTe", sc.copy(out=latT[:, tc * 512:(tc + 1) * 512], in_=pa[tc % 2][:]))

            for io, (wS, kdst) in enumerate([(wblk2[0][:, 0:128], None), (wblk2[0][:, 128:256], None)]):
                for tc in range(4):
                    gk2 = io * 4 + tc
                    @block.tensor
                    def _(te, wS=wS, tc=tc, gk2=gk2):
                        cx.wait(te, "LTe", 4)
                        if gk2 >= 2:
                            cx.wait(te, "KRe", gk2 - 1)
                        for ct in range(CT):
                            mm = te.matmul(out=pa[gk2 % 2][0:16, :], lhsT=wS[:, ct * 16:ct * 16 + 16],
                                           rhs=xaT[ct][:, tc * 512:(tc + 1) * 512],
                                           start=(ct == 0), stop=(ct == CT - 1))
                        cx.inc("KRm", mm)

                    @block.scalar
                    def _(sc, tc=tc, io=io, gk2=gk2):
                        cx.wait(sc, "KRm", gk2 + 1)
                        dstk = krop if io == 0 else krodd
                        cx.inc("KRe", sc.copy(out=dstk[0:16, tc * 512:(tc + 1) * 512],
                                              in_=pa[gk2 % 2][0:16, :]))

            @block.vector
            def _(ve):
                cx.wait(ve, "QFe", 32)
                cx.wait(ve, "KRe", 8)
                cx.wait(ve, "LCS", 48)
                for p in range(2):
                    ev, od = qTf[4 + p], qTf[6 + p]
                    ve.tensor_tensor(out=cvsc[0][:], in0=ev[:], in1=cosS[:], op=ALU.mult)
                    ve.tensor_tensor(out=cvsc[1][:], in0=od[:], in1=sinS[:], op=ALU.mult)
                    ve.tensor_tensor(out=craw[0][:, 0:T], in0=ev[:], in1=sinS[:], op=ALU.mult)
                    ve.tensor_tensor(out=craw[1][:, 0:T], in0=od[:], in1=cosS[:], op=ALU.mult)
                    ve.tensor_tensor(out=ev[:], in0=cvsc[0][:], in1=cvsc[1][:], op=ALU.subtract)
                    ve.tensor_tensor(out=od[:], in0=craw[0][:, 0:T], in1=craw[1][:, 0:T], op=ALU.add)
                ve.tensor_tensor(out=cvsc[0][0:16, :], in0=krop[0:16, :], in1=cosS[0:16, :], op=ALU.mult)
                ve.tensor_tensor(out=cvsc[1][0:16, :], in0=krodd[0:16, :], in1=sinS[0:16, :], op=ALU.mult)
                ve.tensor_tensor(out=craw[0][0:16, 0:T], in0=krop[0:16, :], in1=sinS[0:16, :], op=ALU.mult)
                ve.tensor_tensor(out=craw[1][0:16, 0:T], in0=krodd[0:16, :], in1=cosS[0:16, :], op=ALU.mult)
                ve.tensor_tensor(out=krop[0:16, :], in0=cvsc[0][0:16, :], in1=cvsc[1][0:16, :], op=ALU.subtract)
                ve.tensor_tensor(out=krodd[0:16, :], in0=craw[0][0:16, 0:T], in1=craw[1][0:16, 0:T], op=ALU.add)
                cx.inc("RPD", ve.engine_nop())
            @block.gpsimd
            def _(gp):
                cx.wait(gp, "KRe", 8)
            for tt3 in range(TT):
                @block.tensor
                def _(te, tt3=tt3):
                    cx.wait(te, "LTe", 4)
                    if tt3 == 0:
                        cx.wait(te, "LKV", 16)
                        cx.wait(te, "KRe", 8)
                    if tt3 >= 2:
                        cx.wait(te, "KVe", tt3 - 1)
                    mm = te.matmul(out=pa[tt3 % 2][:], lhsT=latT[:, tt3 * 128:(tt3 + 1) * 128],
                                   rhs=wkvbS[:], start=True, stop=True)
                    cx.inc("KVm", mm)

                @block.vector
                def _(ve, tt3=tt3):
                    cx.wait(ve, "KVm", tt3 + 1)
                    mm = ve.tensor_copy(
                        out=vaug[:, tt3 * 528:(tt3 + 1) * 528].rearrange("p (h d) -> p h d", h=16)[:, :, 0:32],
                        in_=pa[tt3 % 2][:].rearrange("p (h d) -> p h d", h=16))
                    cx.inc("KVe", mm)

            @block.vector
            def _(ve):
                cx.wait(ve, "KVe", TT)
                cx.inc("KVo", ve.memset(vaug[:].rearrange("p (t h d) -> p t h d", t=TT, h=16)[:, :, :, 32:33], 1.0))

            for kt in range(4):
                for tc in range(4):
                    gkt = kt * 4 + tc
                    @block.tensor
                    def _(te, kt=kt, tc=tc, gkt=gkt):
                        if gkt == 0:
                            cx.wait(te, "LTe", 4)
                            cx.wait(te, "LKV", 16)
                            cx.wait(te, "KVe", 16)
                        if gkt >= 2:
                            cx.wait(te, "KTe", gkt - 1)
                        cx.inc("KTm", te.matmul(out=pa[gkt % 2][:], lhsT=wkvbS[:, kt * 128:(kt + 1) * 128],
                                                rhs=latT[:, tc * 512:(tc + 1) * 512], start=True, stop=True))

                    @block.scalar
                    def _(sc, kt=kt, tc=tc, gkt=gkt):
                        if gkt == 0:
                            cx.wait(sc, "RPD", 1)
                        cx.wait(sc, "KTm", gkt + 1)
                        cx.inc("KTe", sc.copy(out=kvT[kt][:, tc * 512:(tc + 1) * 512],
                                              in_=pa[gkt % 2][:]))

        bar()
        if _cut(7): return nc

        with nc.psum_tensor("psc0", [128, 512], F32) as psc0, \
             nc.psum_tensor("psc1", [128, 512], F32) as psc1, \
             nc.psum_tensor("pav", [128, 512], F32) as pav, \
             nc.psum_tensor("pnr2", [128, 512], F32) as pnr2:
            psc = [psc0, psc1]
            for h in range(HD):
                @block.vector
                def _(ve, h=h):
                    if h > 0:
                        cx.wait(ve, "AVe", h * 4)
                    nt = (h * 32) // 128
                    no = (h * 32) % 128
                    cx.bump(ve, "v", ve.tensor_copy(out=qpk[0:32, :], in_=qTf[nt][no:no + 32, :]))
                    kt = (h * 32) // 128
                    cx.bump(ve, "v", ve.tensor_copy(out=kpk[0:32, :], in_=kvT[kt][no:no + 32, :]))
                    cx.bump(ve, "v", ve.tensor_copy(out=kpk[32:48, :], in_=krop[0:16, :]))
                    cx.inc("PKv", ve.engine_nop())

                @block.sync
                def _(sy, h=h):
                    cx.wait(sy, "PKv", h + 1)
                    et = (512 + h * 16) // 128 + 4
                    eo = (512 + h * 16) % 128
                    ot = (1024 + 512 + h * 16) // 128 + 2
                    oo = (512 + 256 + h * 16) % 128
                    cx.inc("PKd", sy.dma_start(out=qpk[32:48, :], in_=qTf[4 + (h * 16) // 128][(h * 16) % 128:(h * 16) % 128 + 16, :]), 16)
                    cx.inc("PKd", sy.dma_start(out=qpk[48:64, :], in_=qTf[6 + (h * 16) // 128][(h * 16) % 128:(h * 16) % 128 + 16, :]), 16)
                    cx.inc("PKd", sy.dma_start(out=kpk[48:64, :], in_=krodd[0:16, :]), 16)

                for tqc in range(4):
                    nblk = (tqc + 1) * 4
                    for tkb in range(nblk):
                        def teF(te, h=h, tqc=tqc, tkb=tkb, nblk=nblk):
                            if h == 0 and tqc == 0 and tkb == 0:
                                cx.wait(te, "RPD", 1)
                                cx.wait(te, "KVo", 1)
                                cx.wait(te, "KTe", 16)
                            cx.wait(te, "av", cx.val("av"))
                            cx.wait(te, "PKd", 48 * (h + 1))
                            m = te.matmul(out=psc0[:], lhsT=kpk[:, tkb * 128:(tkb + 1) * 128],
                                          rhs=qpk[:, tqc * 512:(tqc + 1) * 512], start=True, stop=True)
                            cx.inc("am", m)
                        block.tensor(teF)

                        def scF(sc, tkb=tkb):
                            cx.wait(sc, "am", cx.val("am"))
                            cx.inc("as2", sc.activation(out=pscr[0][:], in_=psc0[:], func=AF.Exp, scale=0.125))
                        block.scalar(scF)

                        r = tkb - 4 * tqc
                        if r >= 0:
                            def veF(ve, r=r):
                                cx.wait(ve, "as2", cx.val("as2"))
                                cx.inc("av", ve.tensor_tensor(out=pscr[0][:], in0=pscr[0][:],
                                                              in1=cm4[:, r * 512:(r + 1) * 512], op=ALU.mult))
                            block.vector(veF)

                        def teF2(te, h=h, tqc=tqc, tkb=tkb, nblk=nblk, r=r):
                            if r >= 0:
                                cx.wait(te, "av", cx.val("av"))
                            else:
                                cx.wait(te, "as2", cx.val("as2"))
                            m = te.matmul(out=pav[0:33, :],
                                          lhsT=vaug[:, tkb * 528 + h * 33:tkb * 528 + h * 33 + 33],
                                          rhs=pscr[0][:], start=(tkb == 0), stop=(tkb == nblk - 1))
                            cx.inc("am", m)
                        block.tensor(teF2)

                    def veN(ve, h=h, tqc=tqc):
                        cx.wait(ve, "am", cx.val("am"))
                        cx.inc("av", ve.tensor_copy(out=yst[:, 0:512], in_=pav[0:33, :].broadcast_to([33, 512]) if False else pav[0:33, :]))
                    block.vector(veN)

                    def syN(sy, h=h, tqc=tqc):
                        cx.wait(sy, "av", cx.val("av"))
                        cx.inc("PKr", sy.dma_start(out=rrow2[0:1, :], in_=yst[32:33, 0:512]), 16)
                    block.sync(syN)

                    def veN2(ve, h=h, tqc=tqc):
                        cx.wait(ve, "PKr", cx.val("PKr"))
                        cx.bump(ve, "v", ve.tensor_copy(out=rrow[0:1, :], in_=rrow2[0:1, :]))
                        cx.bump(ve, "v", ve.reciprocal(out=rrow[0:1, :], in_=rrow[0:1, :]))
                        cx.inc("av", ve.tensor_copy(out=rrowb[0:1, :], in_=rrow[0:1, :]))
                    block.vector(veN2)

                    def teN(te, h=h, tqc=tqc):
                        cx.wait(te, "av", cx.val("av"))
                        m = te.matmul(out=pnr2[:], lhsT=onesr[:], rhs=rrowb[0:1, :], start=True, stop=True)
                        cx.inc("am", m)
                    block.tensor(teN)

                    def veN3(ve, h=h, tqc=tqc):
                        cx.wait(ve, "am", cx.val("am"))
                        ro = (h % 4) * 32
                        cx.inc("AVe", ve.tensor_tensor(out=yeff[h // 4][ro:ro + 32, tqc * 512:(tqc + 1) * 512].bitcast(BF16) if False else yeff[h // 4][ro:ro + 32, tqc * 512:(tqc + 1) * 512],
                                                       in0=yst[0:32, 0:512], in1=pnr2[0:32, :], op=ALU.mult))
                    block.vector(veN3)
        bar()
        if _cut(8): return nc
        _dump([(yf_dbg[i], qkvc[8 + i][:]) for i in range(4)] if debug else [])

        @block.sync
        def _(sy):
            for ct in range(CT):
                cx.inc("LXN2", sy.dma_start(out=qTf[ct][:, 0:T] if False else zT[ct][:], in_=xnt_s[ct * 128:(ct + 1) * 128, :]), 16)
        with nc.psum_tensor("pwo0", [128, 512], F32) as pwo0, \
             nc.psum_tensor("pwo1", [128, 512], F32) as pwo1:
            pwo = [pwo0, pwo1]
            for co in range(CT):
                @block.gpsimd
                def _(gp, co=co):
                    if co >= 2:
                        cx.wait(gp, "WOm2", 4 * (co - 1))
                    cx.inc("LWO2", gp.dma_start(out=wblk[co % 2][:, 0:512], in_=woD[co]), 16)
                for tc in range(4):
                    gw = co * 4 + tc
                    @block.tensor
                    def _(te, co=co, tc=tc, gw=gw):
                        if tc == 0:
                            cx.wait(te, "LWO2", 16 * (co + 1))
                        if gw >= 2:
                            cx.wait(te, "WOe2", 2 * (gw - 1))
                        for yt4 in range(4):
                            mm = te.matmul(out=pwo[gw % 2][:], lhsT=wblk[co % 2][:, yt4 * 128:(yt4 + 1) * 128],
                                           rhs=yeff[yt4][:, tc * 512:(tc + 1) * 512],
                                           start=(yt4 == 0), stop=(yt4 == 3))
                        cx.inc("WOm2", mm)

                    @block.vector
                    def _(ve, co=co, tc=tc, gw=gw):
                        cx.wait(ve, "WOm2", gw + 1)
                        cx.wait(ve, "LXN2", 16 * CT)
                        ve.tensor_tensor(out=outT[co][tc // 2][:, (tc % 2) * 512:(tc % 2) * 512 + 512],
                                         in0=outT[co][tc // 2][:, (tc % 2) * 512:(tc % 2) * 512 + 512],
                                         in1=pwo[gw % 2][:], op=ALU.add)
                        mm = ve.tensor_tensor(out=xaT[co][:, tc * 512:(tc + 1) * 512],
                                              in0=zT[co][:, tc * 512:(tc + 1) * 512],
                                              in1=pwo[gw % 2][:], op=ALU.add)
                        cx.inc("WOe2", mm, 2)
        bar()
        if _cut(9): return nc
        _dump(([(xa_dbg[i], xhT[i][:]) for i in range(8)] + [(o9_dbg[i], qkvc[i][:]) for i in range(8)]) if debug else [])

        tnorm("n2f", xaT, xaT, w_ln2f, 1e-5)
        bar()
        moe("m2", xaT, wgtS, True, wgf, wuf, wdf, shared=True)
        bar()
        if _cut(10): return nc

        with nc.psum_tensor("pf0", [128, 1024], BF16) as pf0, \
             nc.psum_tensor("pf1", [128, 1024], BF16) as pf1:
            pf = [pf0, pf1]
            for tt in range(TT):
                @block.sync
                def _(sy, tt=tt):
                    if tt >= 2:
                        cx.wait(sy, "FST", 16 * (tt - 1))
                    cx.inc("FLB", sy.dma_start(out=obt[tt % 2][:], in_=ob_s[tt * 128:(tt + 1) * 128, :]), 16)

                @block.tensor
                def _(te, tt=tt):
                    for ct in range(CT):
                        gf = tt * CT + ct
                        if gf >= 2:
                            cx.wait(te, "FTe", gf - 1)
                        src = outT[ct][tt // 8][:, (tt % 8) * 128:(tt % 8) * 128 + 128]
                        mm = te.transpose(out=pf[gf % 2][:, 0:128], in_=src, identity=sid[:])
                        cx.inc("FTm", mm)

                @block.vector
                def _(ve, tt=tt):
                    for ct in range(CT):
                        gf = tt * CT + ct
                        cx.wait(ve, "FTm", gf + 1)
                        cx.wait(ve, "FLB", 16 * (tt + 1))
                        mm = ve.tensor_tensor(out=obt[tt % 2][:, ct * 128:(ct + 1) * 128],
                                              in0=obt[tt % 2][:, ct * 128:(ct + 1) * 128],
                                              in1=pf[gf % 2][:, 0:128], op=ALU.add)
                        cx.inc("FTe", mm)

                @block.sync
                def _(sy, tt=tt):
                    cx.wait(sy, "FTe", (tt + 1) * CT)
                    cx.inc("FST", sy.dma_start(out=outD[tt * 128:(tt + 1) * 128, :], in_=obt[tt % 2][:]), 16)

        @block.sync
        def _(sy):
            cx.wait(sy, "FST", 16 * TT)
    return nc


_NC_CACHE = {}


def _get_nc(name, builder):
    if name not in _NC_CACHE:
        _NC_CACHE[name] = builder()
    return _NC_CACHE[name]


def kernel(**inputs):
    cores = list(range(NCORE))

    ncA = _get_nc("A", lambda: build_A(stage=10))
    in_A = [host_prep_A(inputs, c) for c in cores]
    resA = run_bass_kernel_spmd(ncA, in_A, core_ids=cores)

    cnt_all = np.stack([resA.results[c]["cnt_o"][:, 0] for c in cores])
    xn_all = np.concatenate([resA.results[c]["xn_out"] for c in cores])

    ncB = _get_nc("B", build_B)
    in_B = [host_prep_B(c, cnt_all, resA.results[c]["sel_o"], resA.results[c]["eids_o"],
                        resA.results[c]["wts_o"]) for c in cores]
    resB = run_bass_kernel_spmd(ncB, in_B, core_ids=cores)

    tables = np.stack([resB.results[c]["table"][:NE * CAP, 0].reshape(256, 128).T for c in cores])

    ncC = _get_nc("C", build_C)
    in_C = []
    for c in cores:
        my = np.ascontiguousarray(tables[:, :, c * 32:(c + 1) * 32])
        in_C.append(host_prep_C(inputs, c, xn_all, my))
    resC = run_bass_kernel_spmd(ncC, in_C, core_ids=cores)

    y_all = np.ascontiguousarray(np.concatenate([resC.results[c]["y_part"] for c in cores]))

    ncD = _get_nc("D", build_D)
    in_D = [host_prep_D(inputs, c,
                        r1=np.asarray(resA.results[c]["r1"], np.float32),
                        y_all=y_all,
                        islot=np.asarray(resB.results[c]["islot"], np.float32),
                        wfin=np.asarray(resB.results[c]["wfin"], np.float32)) for c in cores]
    resD = run_bass_kernel_spmd(ncD, in_D, core_ids=cores)

    out = np.stack([np.asarray(resD.results[c]["outD"], np.float32) for c in cores])
    return out
